# version 02
# baseline (speedup 1.0000x reference)
import numpy as np
from contextlib import ExitStack

import concourse.bass as bass
import concourse.mybir as mybir
from concourse.bass_utils import run_bass_kernel_spmd

F32 = mybir.dt.float32
BF16 = mybir.dt.bfloat16
AF = mybir.ActivationFunctionType
OP = mybir.AluOpType
AX = mybir.AxisListType

H = 2048
T = 2048
NCT = 16

ENGINES = ("sync", "scalar", "vector", "tensor", "gpsimd")

class _Op:
    __slots__ = ("eng", "fn", "deps", "dma", "needed", "pos")
    def __init__(self, eng, fn, deps, dma, pos):
        self.eng = eng; self.fn = fn; self.deps = deps; self.dma = dma
        self.needed = False; self.pos = pos

class KB:
    debug_full_sync = False
    sync_engines = ('vector',)
    def __init__(self, nc):
        self.nc = nc
        self.ops = []
        self.fences = []

    def fence(self):
        self.fences.append(len(self.ops))

    def op(self, eng, fn, deps=(), dma=False):
        nd = []
        def flat(ds):
            for d in ds:
                if d is None:
                    continue
                elif isinstance(d, _Op):
                    nd.append((d, False))
                elif isinstance(d, tuple):
                    nd.append(d)
                else:
                    flat(d)
        flat(deps)
        for d, _ in nd:
            assert d.pos < len(self.ops), "dep must be an earlier op"
        o = _Op(eng, fn, nd, dma, len(self.ops))
        self.ops.append(o)
        return o

    def dma(self, q, out, in_, deps=(), **kw):
        return self.op(q, lambda e: e.dma_start(out=out, in_=in_, **kw), deps, dma=True)

    def mm(self, out, lhsT, rhs, start, stop, deps=(), tp=None):
        return self.op("tensor",
                       lambda e: e.matmul(out, lhsT, rhs, start=start, stop=stop,
                                          tile_position=tp), deps)

    def trans(self, out, in_, ident, deps=()):
        return self.op("tensor", lambda e: e.transpose(out, in_, ident), deps)

    def act(self, out, in_, func, deps=(), eng="scalar", **kw):
        return self.op(eng, lambda e: e.activation(out, in_, func, **kw), deps)

    def v(self, fn, deps=()):
        return self.op("vector", fn, deps)

    def emit(self, block, ctx):
        NSLOT = 8
        qset = ("sync", "gpsimd", "scalar")
        sems = {e: ctx.enter_context(self.nc.semaphore(f"prog_{e}")) for e in ENGINES}
        dsems = {e: [ctx.enter_context(self.nc.semaphore(f"dma_{e}{j}"))
                     for j in range(NSLOT)] for e in qset}
        for o in self.ops:
            if o.dma:
                o.needed = True
            for d, _ in o.deps:
                d.needed = True
        lasts = {}
        for o in self.ops:
            lasts[o.eng] = o
        for o in lasts.values():
            o.needed = True
        cnt = {e: 0 for e in ENGINES}
        dcnt = {e: 0 for e in qset}
        tgt_of = {}
        issue_wait = {}
        fsnaps = []
        fidx = 0
        fences = sorted(self.fences)

        def snap():
            s = [(e, cnt[e]) for e in ENGINES if cnt[e] > 0]
            for e in qset:
                for j in range(NSLOT):
                    n = dcnt[e] // NSLOT + (1 if dcnt[e] % NSLOT > j else 0)
                    if n > 0:
                        s.append(((e, j), 16 * n))
            return s
        for o in self.ops:
            while fidx < len(fences) and o.pos >= fences[fidx]:
                fsnaps.append(snap())
                fidx += 1
            if not o.needed:
                continue
            if o.dma:
                k = dcnt[o.eng]; dcnt[o.eng] += 1
                j = k % NSLOT
                tgt_of[o.pos] = ((o.eng, j), 16 * (k // NSLOT + 1))
                if k >= NSLOT:
                    issue_wait[o.pos] = ((o.eng, j), 16 * (k // NSLOT))
            else:
                cnt[o.eng] += 1
                tgt_of[o.pos] = (o.eng, cnt[o.eng])
        while fidx < len(fences):
            fsnaps.append(snap())
            fidx += 1
        self.final_counts = [(e, cnt[e]) for e in ENGINES if cnt[e] > 0]
        for e in qset:
            for j in range(NSLOT):
                n = dcnt[e] // NSLOT + (1 if dcnt[e] % NSLOT > j else 0)
                if n > 0:
                    self.final_counts.append(((e, j), 16 * n))

        def sem_of(key):
            return sems[key] if isinstance(key, str) else dsems[key[0]][key[1]]

        fences = sorted(self.fences)

        def make_runner(e):
            def run(eng):
                waited = {}
                fih = [0]
                def wait(key, val):
                    if waited.get(key, 0) < val:
                        eng.wait_ge(sem_of(key), val)
                        waited[key] = val
                for o in self.ops:
                    if o.eng != e:
                        continue
                    while fih[0] < len(fences) and o.pos >= fences[fih[0]]:
                        for key, val in fsnaps[fih[0]]:
                            wait(key, val)
                        fih[0] += 1
                    for d, forced in o.deps:
                        if (d.eng == e and not forced and not d.dma
                                and not self.debug_full_sync
                                and e not in self.sync_engines):
                            continue
                        if d.pos in tgt_of:
                            key, val = tgt_of[d.pos]
                            wait(key, val)
                    if o.pos in issue_wait:
                        key, val = issue_wait[o.pos]
                        wait(key, val)
                    inst = o.fn(eng)
                    if o.needed:
                        key, val = tgt_of[o.pos]
                        inst.then_inc(sem_of(key), 16 if o.dma else 1)
                if e == "sync":
                    for key, val in self.final_counts:
                        wait(key, val)
            return run

        for e in ENGINES:
            getattr(block, e)(make_runner(e))
        return sems


def sbuf(ctx, nc, name, shape, dtype):
    return ctx.enter_context(nc.sbuf_tensor(name, shape, dtype))

def psum_t(ctx, nc, name, shape, dtype=F32):
    return ctx.enter_context(nc.psum_tensor(name, shape, dtype))

class Buf:
    def __init__(self, t):
        self.t = t
        self.writers = []
        self.readers = []
    def war(self):
        return list(self.readers) if self.readers else list(self.writers)
    def begin(self):
        d = self.war()
        self.writers = []
        self.readers = []
        return d
    def wrote(self, op):
        self.writers.append(op)
        return op
    def read(self, op):
        self.readers.append(op)
        return op
    def rdeps(self):
        return list(self.writers)

PS = Buf
SB = Buf


class Arena:
    def __init__(self, ctx, nc, name, words, dtype=BF16):
        self.t = sbuf(ctx, nc, name, [128, words], dtype)
        self.off = 0
        self.words = words
        self.retired = []
    def alloc(self, shape):
        n = int(np.prod(shape[1:]))
        n_pad = (n + 15) // 16 * 16
        off = self.off
        assert off + n <= self.words, f"arena overflow: {off}+{n} > {self.words}"
        self.off = off + n_pad
        ap = self.t[0:shape[0], off:off + n]
        if len(shape) == 3:
            ap = ap.rearrange("p (a b) -> p a b", b=shape[2])
        b = Buf(ap)
        b.writers = list(self.retired)
        return b
    def mark(self):
        return (self.off, list(self.retired))
    def release(self, mk, bufs):
        self.off = mk[0]
        ret = list(mk[1])
        for b in bufs:
            ret.extend(b.readers)
            ret.extend(b.writers)
        ret = sorted(set(ret), key=lambda o: o.pos)[-16:]
        self.retired = ret

class Rows:
    def __init__(self, ctx, nc, name, W, n_max=8, dtype=F32, backing=None):
        self.t_full = backing if backing is not None else sbuf(
            ctx, nc, name, [1, n_max * W], dtype)
        self.W = W
        self.n = 0
        self.n_max = n_max
        self.bufs = {}
    def row(self, key, W=None):
        if key in self.bufs:
            return self.bufs[key]
        W = W or self.W
        assert self.n + W <= self.n_max * self.W
        b = Buf(self.t_full[0:1, self.n:self.n + W])
        self.n += W
        self.bufs[key] = b
        return b

class St:
    def __init__(self, nc, kb, ctx, n_f32_banks=6, n_bf_banks=2, arena_words=0,
                 arena32_words=0):
        self.nc = nc; self.kb = kb; self.ctx = ctx
        self.arena = Arena(ctx, nc, "arena", arena_words) if arena_words else None
        self.arena32 = (Arena(ctx, nc, "arena32", arena32_words, F32)
                        if arena32_words else None)
        self.pb = [PS(psum_t(ctx, nc, f"pbank{i}", [128, 512], F32))
                   for i in range(n_f32_banks)]
        self.pbf = [PS(psum_t(ctx, nc, f"pbankbf{i}", [128, 1024], BF16))
                    for i in range(n_bf_banks)]
        self.consts = {}

    def const(self, name, arr, dt=BF16):
        name = f"{name}_p{getattr(self, 'phase', 0)}"
        if name in self.consts:
            return self.consts[name]
        nc, kb, ctx = self.nc, self.kb, self.ctx
        hbm = nc.inline_tensor(np.ascontiguousarray(arr.astype(np.float32)), name).ap()
        t = sbuf(ctx, nc, name + "_sb", list(arr.shape), dt)
        q = "gpsimd" if dt != F32 else "sync"
        op = kb.dma(q, t[:], hbm[:])
        self.consts[name] = (t, op)
        return self.consts[name]

    def ident(self):
        return self.const("c_ident", np.eye(128, dtype=np.float32), BF16)

    def ones_col(self):
        return self.const("c_ones_col", np.ones((128, 1), np.float32), BF16)

    def ones_row(self):
        return self.const("c_ones_row", np.ones((1, 128), np.float32), BF16)


def load_wt(st, name, hbm_ap, R, C, deps=()):
    t = sbuf(st.ctx, st.nc, name, [128, R // 128, C], BF16)
    s = SB(t)
    ap = hbm_ap.rearrange("(a p) n -> p a n", p=128)
    s.wrote(st.kb.dma("gpsimd", t[:], ap, deps=deps))
    return s


def emit_ln(st, x_hbm, w_row=None, post=None, name="ln", h_tiles=None,
            stat_banks=(0, 1, 2, 3), apply_banks=(0, 1, 2, 3)):
    nc, kb, ctx = st.nc, st.kb, st.ctx
    ln_mark = st.arena.mark() if st.arena else None
    if st.arena:
        xs = [st.arena.alloc([128, T]) for j in range(2)]
    else:
        xs = [SB(sbuf(ctx, nc, f"{name}_xs{j}", [128, T], BF16)) for j in range(2)]
    sq = [SB(sbuf(ctx, nc, f"{name}_sq{j}", [128, 512], BF16)) for j in range(2)]
    r32mark = st.arena32.mark() if st.arena32 else None
    if st.arena32:
        rows_buf = st.arena32.alloc([1, 3 * T])
        rows = Rows(ctx, nc, name + "_rows", T, n_max=3, backing=rows_buf.t)
    else:
        rows_buf = None
        rows = Rows(ctx, nc, name + "_rows", T, n_max=3)
    r_sum = rows.row("sum"); r_sq = rows.row("sumsq"); r_msq = rows.row("msq")
    r_m = r_sum
    r_a = r_sq
    r_b = r_m
    onesb, onesb_ld = st.ones_col()
    pA = st.pb[stat_banks[0]]; pB = st.pb[stat_banks[1]]
    if st.arena:
        xsl = [st.arena.alloc([128, 512]) for j in range(3)]
    else:
        xsl = [SB(sbuf(ctx, nc, f"{name}_xsl{j}", [128, 512], BF16)) for j in range(3)]
    ev = []
    for tq in range(4):
        sl = slice(tq * 512, (tq + 1) * 512)
        warA = pA.begin(); warB = pB.begin()
        mm_s = None; mm_q = None
        for i in range(NCT):
            j = (tq * NCT + i) % 3
            ld = kb.dma("gpsimd", xsl[j].t[:], x_hbm[i * 128:(i + 1) * 128, sl],
                        deps=xsl[j].begin())
            xsl[j].wrote(ld)
            jj = i % 2
            sqo = kb.act(sq[jj].t[:], xsl[j].t[:], AF.Square,
                         deps=[ld] + sq[jj].begin())
            sq[jj].wrote(sqo); xsl[j].read(sqo)
            mm_s = pA.wrote(kb.mm(pA.t[0:1, :], onesb[:, 0:1], xsl[j].t[:],
                                  start=(i == 0), stop=(i == NCT - 1),
                                  deps=[ld, onesb_ld] + (warA if i == 0 else [])))
            xsl[j].read(mm_s)
            mm_q = pB.wrote(kb.mm(pB.t[0:1, :], onesb[:, 0:1], sq[jj].t[:],
                                  start=(i == 0), stop=(i == NCT - 1),
                                  deps=[sqo] + (warB if i == 0 else [])))
            sq[jj].read(mm_q)
        ev.append(pA.read(kb.v(lambda e, sl=sl: e.tensor_copy(
            r_sum.t[0:1, sl], pA.t[0:1, :]),
            pA.rdeps() + (r_sum.begin() if tq == 0 else []))))
        r_sum.wrote(ev[-1])
        ev.append(pB.read(kb.v(lambda e, sl=sl: e.tensor_copy(
            r_sq.t[0:1, sl], pB.t[0:1, :]),
            pB.rdeps() + (r_sq.begin() if tq == 0 else []))))
        r_sq.wrote(ev[-1])
    o1 = kb.v(lambda e: e.tensor_scalar_mul(r_m.t[:], r_sum.t[:], 1.0 / H),
              r_sum.rdeps())
    r_m.wrote(o1)
    o2 = kb.v(lambda e: e.tensor_mul(r_msq.t[:], r_m.t[:], r_m.t[:]),
              [o1] + r_msq.begin())
    r_msq.wrote(o2)
    o3 = kb.v(lambda e: e.scalar_tensor_tensor(r_sq.t[:], r_sq.t[:], 1.0 / H,
                                               r_msq.t[:], OP.mult, OP.subtract),
              [o2] + r_sq.rdeps())
    r_sq.wrote(o3)
    o3b = kb.v(lambda e: e.tensor_scalar_add(r_sq.t[:], r_sq.t[:], 1e-5), [o3])
    r_sq.wrote(o3b)
    o4 = kb.act(r_sq.t[:], r_sq.t[:], AF.Sqrt, deps=[o3b])
    o5 = kb.v(lambda e: e.reciprocal(r_sq.t[:], r_sq.t[:]), [o4])
    r_sq.wrote(o5)
    rows.bufs["inv"] = r_sq
    if post is None:
        o6 = o5
    else:
        o6 = post(st, rows, [o5])
        r_a.wrote(o6)
    o7 = kb.v(lambda e: e.tensor_mul(r_m.t[:], r_m.t[:], r_a.t[:]), [o6, o1])
    o8 = kb.v(lambda e: e.tensor_scalar_mul(r_m.t[:], r_m.t[:], -1.0), [o7])
    r_b.wrote(o8)
    abf = st.arena.alloc([1, 2 * T]) if st.arena else SB(sbuf(ctx, nc, f"{name}_abf", [1, 2 * T], BF16))
    o9 = kb.v(lambda e: e.tensor_copy(abf.t[0:1, 0:T], r_a.t[:]),
              [o6] + abf.begin())
    o10 = kb.v(lambda e: e.tensor_copy(abf.t[0:1, T:2 * T], r_b.t[:]), [o8])
    abf.wrote(o9); abf.wrote(o10)
    if h_tiles is None:
        h_tiles = [SB(sbuf(ctx, nc, f"{name}_h{i}", [128, T], BF16)) for i in range(NCT)]
    if w_row is None:
        wones, wdep = st.ones_row()
        wsl = lambda i: wones[0:1, 0:128]
    else:
        wsl, wdep = w_row
    if st.arena:
        tmp = [st.arena.alloc([128, 512]) for j in range(2)]
    else:
        tmp = [SB(sbuf(ctx, nc, f"{name}_tmp{j}", [128, 512], BF16)) for j in range(2)]
    ab = [st.pb[b] for b in apply_banks]
    for i in range(NCT):
        j = i % 2
        ld = kb.dma("gpsimd", xs[j].t[:], x_hbm[i * 128:(i + 1) * 128, :],
                    deps=xs[j].begin())
        xs[j].wrote(ld)
        hw = h_tiles[i].begin()
        for tq in range(4):
            u = (i * 4 + tq)
            pa = ab[(2 * u) % len(ab)]
            pbb = ab[(2 * u + 1) % len(ab)]
            jj = u % 2
            sl = slice(tq * 512, (tq + 1) * 512)
            mma = pa.wrote(kb.mm(pa.t[:], wsl(i), abf.t[0:1, tq * 512:(tq + 1) * 512],
                                 start=True, stop=True,
                                 deps=[o9, wdep] + pa.begin()))
            mmb = pbb.wrote(kb.mm(pbb.t[:], wsl(i),
                                  abf.t[0:1, T + tq * 512:T + (tq + 1) * 512],
                                  start=True, stop=True,
                                  deps=[o10, wdep] + pbb.begin()))
            t1 = kb.v(lambda e, sl=sl, jj=jj, j=j, pa=pa: e.tensor_mul(
                tmp[jj].t[:], xs[j].t[:, sl], pa.t[:]),
                [mma, ld] + tmp[jj].begin())
            pa.read(t1); tmp[jj].wrote(t1); xs[j].read(t1)
            t2 = kb.v(lambda e, i=i, sl=sl, jj=jj, pbb=pbb: e.tensor_add(
                h_tiles[i].t[:, sl], tmp[jj].t[:], pbb.t[:]), [mmb, t1] + hw)
            pbb.read(t2); tmp[jj].read(t2); h_tiles[i].wrote(t2)
    if st.arena:
        st.arena.release(ln_mark, xs + [abf] + sq + xsl + tmp)
    if st.arena32:
        rb = [rows_buf] if rows_buf is not None else []
        for b in rows.bufs.values():
            rb.append(b)
        st.arena32.release(r32mark, rb)
    return h_tiles, rows




def emit_attn_tqcs(st, qs, ks, vtok, negB_dep, negb_bf, r_rc, recip_bf,
                   p_sb, otmp, ohead, ohw):
    kb = st.kb
    identb, ident_ld = st.ident()
    onesr, onesr_ld = st.ones_row()
    onesc, onesc_ld = st.ones_col()
    a_ = np.arange(128)[:, None]; c_ = np.arange(896)[None, :]
    dmask, dmask_ld = st.const("c_dmask896",
                               np.where(a_ <= c_ - 384, 0.0, -1e9).astype(np.float32))
    for tqc in range(4):
        sl = slice(tqc * 512, (tqc + 1) * 512)
        pO = st.pb[5]
        pSum = st.pb[2]
        warO = pO.begin(); warSum = pSum.begin()
        ntk = 4 * (tqc + 1)
        pv_last = None; sum_last = None
        for i in range(ntk):
            pS = st.pb[3 + (i % 2)]
            mm1 = pS.wrote(kb.mm(pS.t[:], ks.t[:, i * 128:(i + 1) * 128],
                                 qs.t[:, sl], start=True, stop=False,
                                 deps=ks.rdeps() + qs.rdeps() + pS.begin()))
            ks.read(mm1); qs.read(mm1)
            deps_b = [negB_dep, onesr_ld]
            if i >= 4 * tqc:
                dsh = 128 * (i - 4 * tqc)
                mm2 = pS.wrote(kb.mm(pS.t[:], onesr[:], negb_bf.t[0:1, sl],
                                     start=False, stop=False, deps=deps_b))
                negb_bf.read(mm2)
                mm3 = pS.wrote(kb.mm(pS.t[:], identb[:],
                                     dmask[:, 384 - dsh:896 - dsh],
                                     start=False, stop=True, deps=[dmask_ld]))
            else:
                mm3 = pS.wrote(kb.mm(pS.t[:], onesr[:], negb_bf.t[0:1, sl],
                                     start=False, stop=True, deps=deps_b))
                negb_bf.read(mm3)
            jj = i % 3
            ex = kb.act(p_sb[jj].t[:], pS.t[:], AF.Exp,
                        deps=[mm3] + p_sb[jj].begin())
            pS.read(ex); p_sb[jj].wrote(ex)
            sum_last = pSum.wrote(kb.mm(pSum.t[0:1, :], onesc[:, 0:1],
                                        p_sb[jj].t[:], start=(i == 0),
                                        stop=(i == ntk - 1),
                                        deps=[ex, onesc_ld] +
                                        (warSum if i == 0 else [])))
            p_sb[jj].read(sum_last)
            pv_last = pO.wrote(kb.mm(pO.t[:], vtok.t[:, i, :], p_sb[jj].t[:],
                                     start=(i == 0), stop=(i == ntk - 1),
                                     deps=[ex] + vtok.rdeps() +
                                     (warO if i == 0 else [])))
            p_sb[jj].read(pv_last); vtok.read(pv_last)
        c1 = pSum.read(kb.v(lambda e, sl=sl: e.tensor_copy(
            r_rc.t[0:1, sl], pSum.t[0:1, :]),
            [sum_last] + (r_rc.begin() if tqc == 0 else [])))
        r_rc.wrote(c1)
        c2 = kb.v(lambda e, sl=sl: e.reciprocal(r_rc.t[0:1, sl], r_rc.t[0:1, sl]),
                  [c1])
        r_rc.wrote(c2)
        c3 = kb.v(lambda e, sl=sl, tqc=tqc: e.tensor_copy(
            recip_bf.t[0:1, sl], r_rc.t[0:1, sl]),
            [c2] + (recip_bf.begin() if tqc == 0 else []))
        recip_bf.wrote(c3)
        oe = kb.act(otmp[tqc % 2].t[:], pO.t[:], AF.Identity,
                    deps=[pv_last] + otmp[tqc % 2].begin())
        pO.read(oe); otmp[tqc % 2].wrote(oe)
        pR = st.pb[5]
        mmr = pR.wrote(kb.mm(pR.t[:], onesr[:], recip_bf.t[0:1, sl],
                             start=True, stop=True,
                             deps=[c3, onesr_ld] + pR.begin()))
        recip_bf.read(mmr)
        nm = kb.v(lambda e, sl=sl, jj2=tqc % 2, pR=pR: e.tensor_mul(
            ohead.t[:, sl], otmp[jj2].t[:], pR.t[:]), [oe, mmr] + ohw)
        pR.read(nm); otmp[tqc % 2].read(nm); ohead.wrote(nm)

def emit_mha(st, x_hbm, xout_hbm, wq_hbm, bq_hbm, wo_hbm, bo_hbm, of_hbm,
             name="mha"):
    nc, kb, ctx = st.nc, st.kb, st.ctx
    NH = 16
    h, lnrows = emit_ln(st, x_hbm, name=name + "_ln")
    identb, ident_ld = st.ident()
    onesr, onesr_ld = st.ones_row()
    onesc, onesc_ld = st.ones_col()
    a_ = np.arange(128)[:, None]; c_ = np.arange(896)[None, :]
    dmask, dmask_ld = st.const("c_dmask896",
                               np.where(a_ <= c_ - 384, 0.0, -1e9).astype(np.float32))
    wsl = [[SB(sbuf(ctx, nc, f"{name}_w{m}{j}", [128, NCT, 128], BF16))
            for m in range(3)] for j in range(1)]
    bsl = [[SB(sbuf(ctx, nc, f"{name}_b{m}{j}", [128, 1], F32))
            for m in range(3)] for j in range(2)]
    qkv_sb = [SB(sbuf(ctx, nc, f"{name}_qkv{m}", [128, T], BF16)) for m in range(3)]
    vtok = SB(sbuf(ctx, nc, f"{name}_vtok", [128, NCT, 128], BF16))
    p_sb = [SB(sbuf(ctx, nc, f"{name}_p{j}", [128, 512], BF16)) for j in range(3)]
    mr = Rows(ctx, nc, name + "_mr", T, n_max=3)
    r_qn2 = mr.row("qn2"); r_kn2 = mr.row("kn2"); r_rc = mr.row("recip")
    kmax = SB(sbuf(ctx, nc, f"{name}_kmax", [1, 2], F32))
    negb_bf = SB(sbuf(ctx, nc, f"{name}_negbbf", [1, T], BF16))
    recip_bf = SB(sbuf(ctx, nc, f"{name}_recipbf", [1, T], BF16))
    sqt = [SB(sbuf(ctx, nc, f"{name}_sqt{j}", [128, 512], BF16)) for j in range(2)]
    otmp = [SB(sbuf(ctx, nc, f"{name}_otmp{j}", [128, 512], BF16)) for j in range(2)]
    ohead = [SB(sbuf(ctx, nc, f"{name}_oh{j}", [128, T], BF16)) for j in range(2)]
    SC = float(1.0 / np.sqrt(128.0))
    of_stores = []
    for hh in range(NH):
        j = hh % 2
        lds = []
        for m in range(3):
            col0 = m * H + hh * 128
            wb = wsl[0][m]
            o = kb.dma("gpsimd", wb.t[:],
                       wq_hbm[:, col0:col0 + 128].rearrange("(a p) n -> p a n", p=128),
                       deps=wb.begin())
            wb.wrote(o); lds.append(o)
            bb = bsl[j][m]
            o2 = kb.dma("sync", bb.t[:], bq_hbm[col0:col0 + 128, 0:1], deps=bb.begin())
            bb.wrote(o2); lds.append(o2)
        bqs = kb.v(lambda e, j=j: e.tensor_scalar_mul(bsl[j][0].t[:], bsl[j][0].t[:], SC),
                   [lds[1]])
        bsl[j][0].wrote(bqs)
        for m in range(3):
            dst = qkv_sb[m]
            dw = dst.begin()
            for tq in range(4):
                pbk = st.pb[tq % 2]
                sl = slice(tq * 512, (tq + 1) * 512)
                mml = None
                war = pbk.begin()
                for ct in range(NCT):
                    mml = pbk.wrote(kb.mm(
                        pbk.t[:], wsl[0][m].t[:, ct, :], h[ct].t[:, sl],
                        start=(ct == 0), stop=(ct == NCT - 1),
                        deps=[lds[2 * m]] + h[ct].rdeps() + (war if ct == 0 else [])))
                    h[ct].read(mml); wsl[0][m].read(mml)
                bdep = bqs if m == 0 else lds[2 * m + 1]
                ev = kb.act(dst.t[:, sl], pbk.t[:], AF.Identity,
                            bias=bsl[j][m].t[:], scale=(SC if m == 0 else 1.0),
                            deps=[mml, bdep] + dw)
                pbk.read(ev); dst.wrote(ev); bsl[j][m].read(ev)
        qs, ks, vs = qkv_sb
        pn = st.pb[2]
        nrm_ops = []
        for (src_b, rr) in ((qs, r_qn2), (ks, r_kn2)):
            rw = rr.begin()
            for tq in range(4):
                sl = slice(tq * 512, (tq + 1) * 512)
                jj = tq % 2
                sqo = kb.act(sqt[jj].t[:], src_b.t[:, sl], AF.Square,
                             deps=src_b.rdeps() + sqt[jj].begin())
                sqt[jj].wrote(sqo); src_b.read(sqo)
                mmn = pn.wrote(kb.mm(pn.t[0:1, :], onesc[:, 0:1],
                                     sqt[jj].t[:], start=True, stop=True,
                                     deps=[sqo, onesc_ld] + pn.begin()))
                sqt[jj].read(mmn)
                cp = pn.read(kb.v(lambda e, rr=rr, sl=sl: e.tensor_copy(
                    rr.t[0:1, sl], pn.t[0:1, :]), [mmn] + rw))
                rr.wrote(cp)
                nrm_ops.append(cp)
        o1 = kb.v(lambda e: e.reduce_max(kmax.t[0:1, 0:1], r_kn2.t[:], axis=AX.X),
                  r_kn2.rdeps() + kmax.begin())
        kmax.wrote(o1)
        o2 = kb.act(kmax.t[0:1, 1:2], kmax.t[0:1, 0:1], AF.Sqrt, deps=[o1])
        kmax.wrote(o2)
        o3 = kb.act(r_qn2.t[:], r_qn2.t[:], AF.Sqrt, deps=r_qn2.rdeps())
        r_qn2.wrote(o3)
        o4 = kb.op("vector", lambda e: e.tensor_scalar(
            negb_bf.t[:], r_qn2.t[:], kmax.t[0:1, 1:2], -1.0, OP.mult, OP.mult),
            [o2, o3] + negb_bf.begin())
        negb_bf.wrote(o4); kmax.read(o4); r_qn2.read(o4)
        negB_dep = o4
        vtw = vtok.begin()
        vt_ops = []
        for a in range(NCT):
            pb = st.pbf[a % 2]
            tr = pb.wrote(kb.trans(
                pb.t[:, 0:128],
                vs.t[:, a * 128:(a + 1) * 128], identb[:],
                deps=vs.rdeps() + [ident_ld] + pb.begin()))
            vs.read(tr)
            cp = pb.read(kb.v(lambda e, a=a, pb=pb: e.tensor_copy(
                vtok.t[:, a, :], pb.t[:, 0:128]), [tr] + vtw))
            vtok.wrote(cp)
            vt_ops.append(cp)
        ohw = ohead[j].begin()
        emit_attn_tqcs(st, qs, ks, vtok, negB_dep, negb_bf, r_rc, recip_bf,
                       p_sb, otmp, ohead[j], ohw)
        so = kb.dma("sync", of_hbm[hh * 128:(hh + 1) * 128, :], ohead[j].t[:],
                    deps=ohead[j].rdeps())
        ohead[j].read(so)
        of_stores.append(so)
    o_ld = []
    for ki in range(NCT):
        o = kb.dma("sync", h[ki].t[:], of_hbm[ki * 128:(ki + 1) * 128, :],
                   deps=h[ki].begin() + [of_stores[ki]])
        h[ki].wrote(o)
        o_ld.append(o)
    wout = [SB(sbuf(ctx, nc, f"{name}_wo{j}", [128, NCT, 128], BF16)) for j in range(2)]
    bo_t = [SB(sbuf(ctx, nc, f"{name}_bo{j}", [128, 1], F32)) for j in range(2)]
    xf = [SB(sbuf(ctx, nc, f"{name}_xf{j}", [128, 512], F32)) for j in range(2)]
    rtmp = [SB(sbuf(ctx, nc, f"{name}_rt{j}", [128, 512], F32)) for j in range(2)]
    for co in range(NCT):
        j = co % 2
        wo = wout[j]
        ldw = kb.dma("gpsimd", wo.t[:],
                     wo_hbm[:, co * 128:(co + 1) * 128].rearrange("(a p) n -> p a n", p=128),
                     deps=wo.begin())
        wo.wrote(ldw)
        ldb = kb.dma("sync", bo_t[j].t[:], bo_hbm[co * 128:(co + 1) * 128, 0:1],
                     deps=bo_t[j].begin())
        bo_t[j].wrote(ldb)
        for tqc in range(4):
            sl = slice(tqc * 512, (tqc + 1) * 512)
            pbk = st.pb[tqc % 2]
            war = pbk.begin()
            mml = None
            for ki in range(NCT):
                mml = pbk.wrote(kb.mm(pbk.t[:], wo.t[:, ki, :], h[ki].t[:, sl],
                                      start=(ki == 0), stop=(ki == NCT - 1),
                                      deps=[ldw, o_ld[ki]] + (war if ki == 0 else [])))
                h[ki].read(mml); wo.read(mml)
            u = (co * 4 + tqc) % 2
            ev = kb.act(rtmp[u].t[:], pbk.t[:], AF.Identity, bias=bo_t[j].t[:],
                        deps=[mml, ldb] + rtmp[u].begin())
            pbk.read(ev); rtmp[u].wrote(ev); bo_t[j].read(ev)
            lx = kb.dma("sync", xf[u].t[:], x_hbm[co * 128:(co + 1) * 128, sl],
                        deps=xf[u].begin())
            xf[u].wrote(lx)
            ad = kb.v(lambda e, u=u: e.tensor_add(rtmp[u].t[:], rtmp[u].t[:], xf[u].t[:]),
                      [ev, lx])
            rtmp[u].wrote(ad); xf[u].read(ad)
            so = kb.dma("sync", xout_hbm[co * 128:(co + 1) * 128, sl], rtmp[u].t[:],
                        deps=[ad])
            rtmp[u].read(so)

def build_mha_program(trunc=None):
    nc = bass.Bass("TRN2", target_bir_lowering=False, debug=False)
    xin = nc.dram_tensor("xin", [H, T], F32, kind="ExternalInput").ap()
    wq = nc.dram_tensor("wqkvt", [H, 3 * H], F32, kind="ExternalInput").ap()
    bq = nc.dram_tensor("bqkv", [3 * H, 1], F32, kind="ExternalInput").ap()
    wo = nc.dram_tensor("woutt", [H, H], F32, kind="ExternalInput").ap()
    bo = nc.dram_tensor("bout", [H, 1], F32, kind="ExternalInput").ap()
    xout = nc.dram_tensor("xout", [H, T], F32, kind="ExternalOutput").ap()
    of = nc.dram_tensor("mha_of", [H, T], BF16, kind="Internal").ap()
    ctx = ExitStack()
    with ctx:
        kb = KB(nc)
        st = St(nc, kb, ctx)
        emit_mha(st, xin, xout, wq, bq, wo, bo, of)
        if trunc is not None:
            kb.ops = kb.ops[:trunc]
        with nc.Block() as block:
            kb.emit(block, ctx)
    return nc


def emit_l0ffn(st, x_hbm, xout_hbm, wg_hbm, wq_hbm, name="mf"):
    nc, kb, ctx = st.nc, st.kb, st.ctx
    NH, HD = 16, 128
    SC = float(1.0 / np.sqrt(128.0))
    h, lnrows = emit_ln(st, x_hbm, name=name + "_ln")
    identb, ident_ld = st.ident()
    onesc, onesc_ld = st.ones_col()
    wg = SB(sbuf(ctx, nc, f"{name}_wg", [128, NCT, 16], BF16))
    wg.wrote(kb.dma("gpsimd", wg.t[:], wg_hbm.rearrange("(a p) n -> p a n", p=128)))
    ind_t = [SB(sbuf(ctx, nc, f"{name}_ind{tt}", [128, 16], F32)) for tt in range(NCT)]
    itp_t = [SB(sbuf(ctx, nc, f"{name}_itp{tt}", [128, 16], F32)) for tt in range(NCT)]
    grow = SB(sbuf(ctx, nc, f"{name}_grow", [128, 4 * 16], F32))
    gexp = SB(sbuf(ctx, nc, f"{name}_gexp", [128, 16], F32))
    for tt in range(NCT):
        pbk = st.pb[tt % 2]
        sl = slice(tt * 128, (tt + 1) * 128)
        war = pbk.begin()
        mml = None
        for ct in range(NCT):
            mml = pbk.wrote(kb.mm(pbk.t[:, 0:16], h[ct].t[:, sl], wg.t[:, ct, :],
                                  start=(ct == 0), stop=(ct == NCT - 1),
                                  deps=h[ct].rdeps() + wg.rdeps() +
                                  (war if ct == 0 else [])))
            h[ct].read(mml); wg.read(mml)
        gw = grow.begin()
        o1 = grow.wrote(pbk.read(kb.v(lambda e, pbk=pbk: e.reduce_max(
            grow.t[:, 0:1], pbk.t[:, 0:16], axis=AX.X), [mml] + gw)))
        o2 = grow.wrote(kb.v(lambda e: e.tensor_scalar_mul(
            grow.t[:, 1:2], grow.t[:, 0:1], -1.0), [o1]))
        o3 = gexp.wrote(pbk.read(kb.act(gexp.t[:], pbk.t[:, 0:16], AF.Exp,
                                        bias=grow.t[:, 1:2],
                                        deps=[o2, mml] + gexp.begin())))
        o4 = grow.wrote(kb.v(lambda e: e.reduce_sum(
            grow.t[:, 2:3], gexp.t[:], axis=AX.X), [o3]))
        o5 = grow.wrote(kb.v(lambda e: e.reciprocal(
            grow.t[:, 3:4], grow.t[:, 2:3]), [o4]))
        o6 = ind_t[tt].wrote(pbk.read(kb.v(lambda e, tt=tt, pbk=pbk: e.tensor_single_scalar(
            ind_t[tt].t[:], pbk.t[:, 0:16], grow.t[:, 0:1], OP.is_ge),
            [(o1, True), mml] + ind_t[tt].begin())))
        o7 = itp_t[tt].wrote(kb.v(lambda e, tt=tt: e.tensor_scalar_mul(
            itp_t[tt].t[:], ind_t[tt].t[:], grow.t[:, 3:4]),
            [(o5, True), o6] + itp_t[tt].begin()))
        grow.read(o6); grow.read(o7)
    wsl = [SB(sbuf(ctx, nc, f"{name}_wkv{m}", [128, NCT, 128], BF16)) for m in range(2)]
    kv_sb = [SB(sbuf(ctx, nc, f"{name}_kv{m}", [128, T], BF16)) for m in range(2)]
    kv_tok = [SB(sbuf(ctx, nc, f"{name}_kvt{m}", [128, NCT, 128], BF16)) for m in range(2)]
    ksh_tok = SB(sbuf(ctx, nc, f"{name}_ksht", [128, NCT, 128], BF16))
    vsh_tok = SB(sbuf(ctx, nc, f"{name}_vsht", [128, NCT, 128], BF16))
    for hh in range(NH):
        for m in range(2):
            col0 = (m + 1) * H + hh * 128
            wb = wsl[m]
            ldw = wb.wrote(kb.dma("gpsimd", wb.t[:],
                                  wq_hbm[:, col0:col0 + 128].rearrange("(a p) n -> p a n", p=128),
                                  deps=wb.begin()))
            dst = kv_sb[m]
            dw = dst.begin()
            for tq in range(4):
                pbk = st.pb[tq % 2]
                sl = slice(tq * 512, (tq + 1) * 512)
                war = pbk.begin()
                mml = None
                for ct in range(NCT):
                    mml = pbk.wrote(kb.mm(pbk.t[:], wb.t[:, ct, :], h[ct].t[:, sl],
                                          start=(ct == 0), stop=(ct == NCT - 1),
                                          deps=[ldw] + h[ct].rdeps() +
                                          (war if ct == 0 else [])))
                    h[ct].read(mml); wb.read(mml)
                ev = kb.act(dst.t[:, sl], pbk.t[:], AF.Identity,
                            deps=[mml] + dw)
                pbk.read(ev); dst.wrote(ev)
            tk = kv_tok[m]
            tkw = tk.begin()
            for a in range(NCT):
                pb = st.pbf[a % 2]
                tr = pb.wrote(kb.trans(pb.t[:, 0:128],
                                       dst.t[:, a * 128:(a + 1) * 128], identb[:],
                                       deps=dst.rdeps() + [ident_ld] + pb.begin()))
                dst.read(tr)
                cp = pb.read(kb.v(lambda e, a=a, pb=pb, tk=tk: e.tensor_copy(
                    tk.t[:, a, :], pb.t[:, 0:128]), [tr] + tkw))
                tk.wrote(cp)
            sel = ksh_tok if m == 0 else vsh_tok
            it = ind_t if m == 0 else itp_t
            selw = sel.begin() if hh == 0 else []
            for a in range(NCT):
                if hh == 0:
                    op = kb.v(lambda e, a=a, tk=tk, sel=sel, it=it, hh=hh: e.tensor_scalar_mul(
                        sel.t[:, a, :], tk.t[:, a, :], it[a].t[:, hh:hh + 1]),
                        tk.rdeps() + it[a].rdeps() + selw)
                else:
                    op = kb.v(lambda e, a=a, tk=tk, sel=sel, it=it, hh=hh: e.scalar_tensor_tensor(
                        sel.t[:, a, :], tk.t[:, a, :], it[a].t[:, hh:hh + 1],
                        sel.t[:, a, :], OP.mult, OP.add),
                        tk.rdeps() + it[a].rdeps() + sel.rdeps())
                sel.wrote(op); tk.read(op); it[a].read(op)
    ksh_fm = SB(sbuf(ctx, nc, f"{name}_kshf", [128, T], BF16))
    kfw = ksh_fm.begin()
    for a in range(NCT):
        pb = st.pbf[a % 2]
        tr = pb.wrote(kb.trans(pb.t[:, 0:128],
                               ksh_tok.t[:, a, :], identb[:],
                               deps=ksh_tok.rdeps() + [ident_ld] + pb.begin()))
        ksh_tok.read(tr)
        cp = pb.read(kb.v(lambda e, a=a, pb=pb: e.tensor_copy(
            ksh_fm.t[:, a * 128:(a + 1) * 128], pb.t[:, 0:128]),
            [tr] + kfw))
        ksh_fm.wrote(cp)
    mr = Rows(ctx, nc, name + "_mr", T, n_max=2)
    r_qn2 = mr.row("qn2"); r_rc = mr.row("recip")
    kmax = SB(sbuf(ctx, nc, f"{name}_kmax", [1, 8], F32))
    negb_bf = SB(sbuf(ctx, nc, f"{name}_negbbf", [1, T], BF16))
    recip_bf = SB(sbuf(ctx, nc, f"{name}_recipbf", [1, T], BF16))
    sqt = [SB(sbuf(ctx, nc, f"{name}_sqt{j}", [128, 512], BF16)) for j in range(2)]
    pn = st.pb[2]
    kn_ops = []
    kmw = kmax.begin()
    for tq in range(4):
        sl = slice(tq * 512, (tq + 1) * 512)
        jj = tq % 2
        sqo = kb.act(sqt[jj].t[:], ksh_fm.t[:, sl], AF.Square,
                     deps=ksh_fm.rdeps() + sqt[jj].begin())
        sqt[jj].wrote(sqo); ksh_fm.read(sqo)
        mmn = pn.wrote(kb.mm(pn.t[0:1, :], onesc[:, 0:1], sqt[jj].t[:],
                             start=True, stop=True,
                             deps=[sqo, onesc_ld] + pn.begin()))
        sqt[jj].read(mmn)
        kn_ops.append(kmax.wrote(pn.read(kb.v(lambda e, tq=tq, pn=pn: e.reduce_max(
            kmax.t[0:1, tq:tq + 1], pn.t[0:1, :], axis=AX.X), [mmn] + kmw))))
    k1 = kmax.wrote(kb.v(lambda e: e.reduce_max(kmax.t[0:1, 4:5], kmax.t[0:1, 0:4],
                                                axis=AX.X), kn_ops))
    k2 = kmax.wrote(kb.act(kmax.t[0:1, 5:6], kmax.t[0:1, 4:5], AF.Sqrt, deps=[k1]))
    wqh = SB(sbuf(ctx, nc, f"{name}_wq", [128, NCT, 128], BF16))
    q_sb = SB(sbuf(ctx, nc, f"{name}_q", [128, T], BF16))
    p_sb = [SB(sbuf(ctx, nc, f"{name}_p{j}", [128, 512], BF16)) for j in range(3)]
    otmp = [SB(sbuf(ctx, nc, f"{name}_ot{j}", [128, 512], BF16)) for j in range(2)]
    ohead = [SB(sbuf(ctx, nc, f"{name}_oh{j}", [128, T], BF16)) for j in range(2)]
    xf = [SB(sbuf(ctx, nc, f"{name}_xf{j}", [128, 512], F32)) for j in range(2)]
    rtmp = [SB(sbuf(ctx, nc, f"{name}_rt{j}", [128, 512], F32)) for j in range(2)]
    for hh in range(NH):
        j = hh % 2
        ldw = wqh.wrote(kb.dma("gpsimd", wqh.t[:],
                               wq_hbm[:, hh * 128:hh * 128 + 128].rearrange("(a p) n -> p a n", p=128),
                               deps=wqh.begin()))
        dw = q_sb.begin()
        for tq in range(4):
            pbk = st.pb[tq % 2]
            sl = slice(tq * 512, (tq + 1) * 512)
            war = pbk.begin()
            mml = None
            for ct in range(NCT):
                mml = pbk.wrote(kb.mm(pbk.t[:], wqh.t[:, ct, :], h[ct].t[:, sl],
                                      start=(ct == 0), stop=(ct == NCT - 1),
                                      deps=[ldw] + h[ct].rdeps() +
                                      (war if ct == 0 else [])))
                h[ct].read(mml); wqh.read(mml)
            ev = kb.act(q_sb.t[:, sl], pbk.t[:], AF.Identity, scale=SC,
                        deps=[mml] + dw)
            pbk.read(ev); q_sb.wrote(ev)
        qn_ops = []
        qw = r_qn2.begin()
        for tq in range(4):
            sl = slice(tq * 512, (tq + 1) * 512)
            jj = tq % 2
            sqo = kb.act(sqt[jj].t[:], q_sb.t[:, sl], AF.Square,
                         deps=q_sb.rdeps() + sqt[jj].begin())
            sqt[jj].wrote(sqo); q_sb.read(sqo)
            mmn = pn.wrote(kb.mm(pn.t[0:1, :], onesc[:, 0:1], sqt[jj].t[:],
                                 start=True, stop=True,
                                 deps=[sqo, onesc_ld] + pn.begin()))
            sqt[jj].read(mmn)
            cp = r_qn2.wrote(pn.read(kb.v(lambda e, sl=sl, pn=pn: e.tensor_copy(
                r_qn2.t[0:1, sl], pn.t[0:1, :]), [mmn] + qw)))
            qn_ops.append(cp)
        o3 = kb.act(r_qn2.t[:], r_qn2.t[:], AF.Sqrt, deps=qn_ops)
        r_qn2.wrote(o3)
        o4 = kb.op("vector", lambda e: e.tensor_scalar(
            negb_bf.t[:], r_qn2.t[:], kmax.t[0:1, 5:6], -1.0, OP.mult, OP.mult),
            [k2, o3] + negb_bf.begin())
        negb_bf.wrote(o4); kmax.read(o4); r_qn2.read(o4)
        ohw = ohead[j].begin()
        emit_attn_tqcs(st, q_sb, ksh_fm, vsh_tok, o4, negb_bf, r_rc, recip_bf,
                       p_sb, otmp, ohead[j], ohw)
        for tqc in range(4):
            sl = slice(tqc * 512, (tqc + 1) * 512)
            u = tqc % 2
            lx = xf[u].wrote(kb.dma("sync", xf[u].t[:], x_hbm[hh * 128:(hh + 1) * 128, sl],
                                    deps=xf[u].begin()))
            ad = kb.v(lambda e, u=u, sl=sl, j=j: e.tensor_add(
                rtmp[u].t[:], ohead[j].t[:, sl], xf[u].t[:]),
                [lx] + ohead[j].rdeps() + rtmp[u].begin())
            rtmp[u].wrote(ad); xf[u].read(ad); ohead[j].read(ad)
            so = kb.dma("sync", xout_hbm[hh * 128:(hh + 1) * 128, sl], rtmp[u].t[:],
                        deps=[ad])
            rtmp[u].read(so)

def build_l0ffn_program(trunc=None):
    nc = bass.Bass("TRN2", target_bir_lowering=False, debug=False)
    xin = nc.dram_tensor("xin", [H, T], F32, kind="ExternalInput").ap()
    wg = nc.dram_tensor("wgt", [H, 16], F32, kind="ExternalInput").ap()
    wq = nc.dram_tensor("wqkvt", [H, 3 * H], F32, kind="ExternalInput").ap()
    xout = nc.dram_tensor("xout", [H, T], F32, kind="ExternalOutput").ap()
    ctx = ExitStack()
    with ctx:
        kb = KB(nc)
        st = St(nc, kb, ctx)
        emit_l0ffn(st, xin, xout, wg, wq)
        if trunc is not None:
            kb.ops = kb.ops[:trunc]
        with nc.Block() as block:
            kb.emit(block, ctx)
    return nc

NI = 1408
NIT = 11
ISH = 2816
ISHT = 22

def emit_l1ffn(st, x_hbm, xout_hbm, W, name="moe"):
    nc, kb, ctx = st.nc, st.kb, st.ctx
    ar = st.arena
    onesr, onesr_ld = st.ones_row()
    onesc, onesc_ld = st.ones_col()
    identb, ident_ld = st.ident()
    amark = ar.mark()
    h = [ar.alloc([128, T]) for _ in range(NCT)]
    emit_ln(st, x_hbm, name=name + "_ln", h_tiles=h)
    e8 = np.zeros((8, 128), np.float32)
    for m_ in range(128):
        e8[(m_ % 64) // 8, m_] = 1.0
    E8, E8_ld = st.const("c_e8_128", e8)
    sel = np.zeros((8, 8 * 128), np.float32)
    for f in range(8):
        sel[f, f * 128:(f + 1) * 128] = 1.0
    SEL, SEL_ld = st.const("c_sel8", sel)
    gwp = ar.alloc([128, NCT, 64])
    gwp.wrote(kb.dma("gpsimd", gwp.t[:], W["gwpT"].rearrange("(a p) n -> p a n", p=128),
                     deps=gwp.begin()))
    wd_t = [SB(sbuf(ctx, nc, f"{name}_wd{tt}", [128, 64], F32)) for tt in range(NCT)]
    soft_t = [SB(sbuf(ctx, nc, f"{name}_soft{tt}", [128, 64], BF16)) for tt in range(NCT)]
    scal_t = [SB(sbuf(ctx, nc, f"{name}_sct{tt}", [128, 8], BF16)) for tt in range(NCT)]
    grow = SB(sbuf(ctx, nc, f"{name}_grow", [128, 40], F32))
    etmp = SB(sbuf(ctx, nc, f"{name}_etmp", [128, 64], F32))
    swork = SB(sbuf(ctx, nc, f"{name}_swork", [128, 64], F32))
    indt = SB(sbuf(ctx, nc, f"{name}_indt", [128, 64], F32))
    g8 = SB(sbuf(ctx, nc, f"{name}_g8", [128, 8], F32))
    def r3(buf, b):
        return buf.t[:].rearrange("p (a b) -> p a b", b=b)
    for tt in range(NCT):
        pbk = st.pb[tt % 2]
        sl = slice(tt * 128, (tt + 1) * 128)
        war = pbk.begin()
        mml = None
        for ct in range(NCT):
            mml = pbk.wrote(kb.mm(pbk.t[:, 0:64], h[ct].t[:, sl], gwp.t[:, ct, :],
                                  start=(ct == 0), stop=(ct == NCT - 1),
                                  deps=h[ct].rdeps() + gwp.rdeps() +
                                  (war if ct == 0 else [])))
            h[ct].read(mml); gwp.read(mml)
        gw_ = grow.begin()
        o1 = grow.wrote(pbk.read(kb.v(lambda e, pbk=pbk: e.reduce_max(
            grow.t[:, 0:1], pbk.t[:, 0:64], axis=AX.X), [mml] + gw_)))
        o2 = grow.wrote(kb.v(lambda e: e.tensor_scalar_mul(
            grow.t[:, 1:2], grow.t[:, 0:1], -1.0), [o1]))
        o3 = etmp.wrote(pbk.read(kb.act(etmp.t[:], pbk.t[:, 0:64], AF.Exp,
                                        bias=grow.t[:, 1:2],
                                        deps=[o2, mml] + etmp.begin())))
        o4 = grow.wrote(kb.v(lambda e: e.reduce_sum(
            grow.t[:, 2:3], etmp.t[:], axis=AX.X), [o3]))
        o5 = grow.wrote(kb.v(lambda e: e.reciprocal(grow.t[:, 3:4], grow.t[:, 2:3]),
                             [o4]))
        o6 = etmp.wrote(kb.v(lambda e: e.tensor_scalar_mul(
            etmp.t[:], etmp.t[:], grow.t[:, 3:4]), [o5, o3]))
        o7 = swork.wrote(kb.v(lambda e: e.tensor_copy(swork.t[:], etmp.t[:]),
                              [o6] + swork.begin()))
        prev = o7
        for it5 in range(5):
            m_ = grow.wrote(kb.v(lambda e, it5=it5: e.reduce_max(
                grow.t[:, 4 + it5:5 + it5], swork.t[:], axis=AX.X), [prev]))
            i_ = indt.wrote(kb.v(lambda e, it5=it5: e.tensor_single_scalar(
                indt.t[:], swork.t[:], grow.t[:, 4 + it5:5 + it5], OP.is_ge),
                [m_] + indt.begin()))
            prev = swork.wrote(kb.v(lambda e: e.scalar_tensor_tensor(
                swork.t[:], indt.t[:], -1e9, swork.t[:], OP.mult, OP.add), [i_]))
        thr = grow.wrote(kb.v(lambda e: e.reduce_max(
            grow.t[:, 9:10], swork.t[:], axis=AX.X), [prev]))
        wd = wd_t[tt]
        o8 = wd.wrote(kb.v(lambda e, wd=wd: e.scalar_tensor_tensor(
            wd.t[:], etmp.t[:], grow.t[:, 9:10], etmp.t[:], OP.is_ge, OP.mult),
            [thr, o6] + wd.begin()))
        sc = scal_t[tt]
        o9a = g8.wrote(kb.v(lambda e, wd=wd: e.reduce_sum(
            r3(g8, 1), r3(wd, 8), axis=AX.X), [o8] + g8.begin()))
        o9 = sc.wrote(kb.v(lambda e, sc=sc: e.tensor_copy(sc.t[:], g8.t[:]),
                           [o9a] + sc.begin()))
        g8.read(o9)
        oa = indt.wrote(kb.v(lambda e, wd=wd: e.tensor_single_scalar(
            indt.t[:], wd.t[:], 0.0, OP.is_equal), [o8] + indt.begin()))
        ob = swork.wrote(kb.v(lambda e, wd=wd: e.scalar_tensor_tensor(
            swork.t[:], indt.t[:], -1e9, wd.t[:], OP.mult, OP.add),
            [oa, o8] + swork.begin()))
        oc = g8.wrote(kb.v(lambda e: e.reduce_max(
            r3(g8, 1), r3(swork, 8), axis=AX.X), [ob] + g8.begin()))
        od = swork.wrote(kb.v(lambda e: e.tensor_tensor(
            r3(swork, 8), r3(swork, 8), r3(g8, 1).broadcast_to([128, 8, 8]),
            OP.subtract), [oc]))
        oe = etmp.wrote(kb.act(etmp.t[:], swork.t[:], AF.Exp, deps=[od] + etmp.begin()))
        of = g8.wrote(kb.v(lambda e: e.reduce_sum(
            r3(g8, 1), r3(etmp, 8), axis=AX.X), [oe]))
        og = g8.wrote(kb.v(lambda e: e.reciprocal(g8.t[:], g8.t[:]), [of]))
        sf = soft_t[tt]
        oh = sf.wrote(kb.v(lambda e, sf=sf: e.tensor_tensor(
            r3(sf, 8), r3(etmp, 8), r3(g8, 1).broadcast_to([128, 8, 8]),
            OP.mult), [og, oe] + sf.begin()))
        etmp.read(oh); g8.read(oh)
    gu_stores = []
    softT_stores = []
    qbd_stores = []
    scal8 = ar.alloc([8, T])
    softf_buf = ar.alloc([8, T])
    srw = scal8.begin()
    for f in range(8):
        sfw = softf_buf.begin()
        for tt in range(NCT):
            pb = st.pbf[tt % 2]
            tr = pb.wrote(kb.trans(pb.t[0:8, 0:128],
                                   soft_t[tt].t[:, f * 8:(f + 1) * 8], identb[:],
                                   deps=soft_t[tt].rdeps() + [ident_ld] + pb.begin()))
            soft_t[tt].read(tr)
            cp = pb.read(kb.v(lambda e, tt=tt, pb=pb: e.tensor_copy(
                softf_buf.t[0:8, tt * 128:(tt + 1) * 128], pb.t[0:8, 0:128]),
                [tr] + sfw))
            softf_buf.wrote(cp)
        so = kb.dma("sync", W["softT_scr"][0:8, f * T:(f + 1) * T], softf_buf.t[:],
                    deps=softf_buf.rdeps())
        softf_buf.read(so)
        softT_stores.append(so)
    for tt in range(NCT):
        pb2 = st.pbf[tt % 2]
        tr2 = pb2.wrote(kb.trans(pb2.t[0:8, 0:128], scal_t[tt].t[:], identb[:],
                                 deps=scal_t[tt].rdeps() + [ident_ld] + pb2.begin()))
        scal_t[tt].read(tr2)
        cp2 = pb2.read(kb.v(lambda e, tt=tt, pb2=pb2: e.tensor_copy(
            scal8.t[0:8, tt * 128:(tt + 1) * 128], pb2.t[0:8, 0:128]), [tr2] + srw))
        scal8.wrote(cp2)
    cmark = ar.mark()
    wstr = [ar.alloc([128, NCT, 128]) for j in range(2)]
    gmark = ar.mark()
    gtmp = [ar.alloc([128, 512]) for j in range(2)]
    utmp = [ar.alloc([128, 512]) for j in range(2)]
    gub = [ar.alloc([128, T]) for j in range(2)]
    def gate_up_unit(itv, wg_ap, wu_ap, extra_g=None, extra_u=None):
        j2 = 0
        wgt_ = wstr[0]; wut_ = wstr[1]
        lg = wgt_.wrote(kb.dma("gpsimd", wgt_.t[:], wg_ap, deps=wgt_.begin()))
        lu = wut_.wrote(kb.dma("gpsimd", wut_.t[:], wu_ap, deps=wut_.begin()))
        g = gub[j2]
        gw_ = g.begin()
        for tq in range(4):
            sl = slice(tq * 512, (tq + 1) * 512)
            pG = st.pb[0 + tq % 2]; pU = st.pb[2 + tq % 2]
            warG = pG.begin(); warU = pU.begin()
            mmg = mmu = None
            for ct in range(NCT):
                last = (ct == NCT - 1)
                mmg = pG.wrote(kb.mm(pG.t[:], wgt_.t[:, ct, :], h[ct].t[:, sl],
                                     start=(ct == 0),
                                     stop=(last and extra_g is None),
                                     deps=[lg] + h[ct].rdeps() + (warG if ct == 0 else [])))
                h[ct].read(mmg); wgt_.read(mmg)
                mmu = pU.wrote(kb.mm(pU.t[:], wut_.t[:, ct, :], h[ct].t[:, sl],
                                     start=(ct == 0),
                                     stop=(last and extra_u is None),
                                     deps=[lu] + h[ct].rdeps() + (warU if ct == 0 else [])))
                h[ct].read(mmu); wut_.read(mmu)
            if extra_g is not None:
                mmg = extra_g(pG, tq)
            if extra_u is not None:
                mmu = extra_u(pU, tq)
            ge = kb.act(gtmp[tq % 2].t[:], pG.t[:], AF.Silu,
                        deps=[mmg] + gtmp[tq % 2].begin())
            pG.read(ge); gtmp[tq % 2].wrote(ge)
            ue = kb.act(utmp[tq % 2].t[:], pU.t[:], AF.Identity,
                        deps=[mmu] + utmp[tq % 2].begin())
            pU.read(ue); utmp[tq % 2].wrote(ue)
            gm = kb.v(lambda e, sl=sl, j3=tq % 2, g=g: e.tensor_mul(
                g.t[:, sl], gtmp[j3].t[:], utmp[j3].t[:]), [ge, ue] + gw_)
            gtmp[tq % 2].read(gm); utmp[tq % 2].read(gm); g.wrote(gm)
        return g

    for itv in range(ISHT):
        c0 = itv * 128
        g = gate_up_unit(itv,
                         W["shgT"][:, c0:c0 + 128].rearrange("(a p) n -> p a n", p=128),
                         W["shuT"][:, c0:c0 + 128].rearrange("(a p) n -> p a n", p=128))
        so = kb.dma("sync", W["gu_scr"][c0:c0 + 128, :], g.t[:], deps=g.rdeps())
        g.read(so)
        gu_stores.append(so)

    ar.release(gmark, gub)
    qgu = ar.alloc([128, NCT, 128])
    qad = ar.alloc([128, NIT, 64])
    muw = ar.alloc([128, NCT, 8])
    qbgu = [ar.alloc([64, NI]) for m in range(2)]
    qbd_sb = ar.alloc([64, H])
    scrow = ar.alloc([1, H])
    awd_buf = ar.alloc([64, T])
    gmt = [ar.alloc([8, NI]) for m in range(2)]
    qam_sb = SB(sbuf(ctx, nc, f"{name}_qam", [8, 128], BF16))
    a_sb = ar.alloc([128, T])
    aw_sb = ar.alloc([128, T])
    aw_u = ar.alloc([64, T])
    scbc = [SB(sbuf(ctx, nc, f"{name}_scbc{j}", [128, 512], BF16)) for j in range(2)]
    amark2 = ar.mark()
    gu_f = ar.alloc([128, NIT, T])
    for f in range(8):
        l1 = qgu.wrote(kb.dma("gpsimd", qgu.t[:],
                              W["qaguT"][f].rearrange("(a p) n -> p a n", p=128),
                              deps=qgu.begin()))
        l2 = qad.wrote(kb.dma("gpsimd", qad.t[:],
                              W["qadT"][f].rearrange("(a p) n -> p a n", p=128),
                              deps=qad.begin()))
        l3 = muw.wrote(kb.dma("gpsimd", muw.t[:],
                              W["muw"][f].rearrange("(a p) n -> p a n", p=128),
                              deps=muw.begin()))
        l4 = softf_buf.wrote(kb.dma("sync", softf_buf.t[:],
                                    W["softT_scr"][0:8, f * T:(f + 1) * T],
                                    deps=softf_buf.begin() + [softT_stores[f]]))
        lq = [qbgu[m].wrote(kb.dma("gpsimd", qbgu[m].t[:], W[nm][f],
                                   deps=qbgu[m].begin()))
              for m, nm in ((0, "qbg"), (1, "qbu"))]
        lqd = qbd_sb.wrote(kb.dma("gpsimd", qbd_sb.t[:], W["qbd"][f],
                                  deps=qbd_sb.begin()))
        psc_ops = []
        for (qb_b, nm, n_) in ((qbgu[0], "gsc", NI), (qbgu[1], "usc", NI),
                               (qbd_sb, "dsc", H)):
            lsc = scrow.wrote(kb.dma("gpsimd", scrow.t[0:1, 0:n_], W[nm][f:f + 1, :],
                                     deps=scrow.begin()))
            for cch in range((n_ + 511) // 512):
                w_ = min(512, n_ - cch * 512)
                pR = st.pb[4]
                mmb = pR.wrote(kb.mm(pR.t[0:64, 0:w_], onesr[0:1, 0:64],
                                     scrow.t[0:1, cch * 512:cch * 512 + w_],
                                     start=True, stop=True,
                                     deps=[lsc, onesr_ld] + pR.begin()))
                scrow.read(mmb)
                op_ = kb.v(lambda e, qb_b=qb_b, cch=cch, w_=w_, pR=pR: e.tensor_mul(
                    qb_b.t[0:64, cch * 512:cch * 512 + w_],
                    qb_b.t[0:64, cch * 512:cch * 512 + w_], pR.t[0:64, 0:w_]),
                    [mmb] + qb_b.rdeps())
                pR.read(op_); qb_b.wrote(op_)
                psc_ops.append(op_)
        pQ = st.pb[5]
        warQ = pQ.begin()
        mmq = None
        for ct in range(NCT):
            mmq = pQ.wrote(kb.mm(pQ.t[0:8, 0:128], muw.t[:, ct, :], qgu.t[:, ct, :],
                                 start=(ct == 0), stop=(ct == NCT - 1),
                                 deps=[l1, l3] + (warQ if ct == 0 else [])))
        cq = pQ.read(kb.v(lambda e, pQ=pQ: e.tensor_copy(qam_sb.t[:], pQ.t[0:8, 0:128]),
                          [mmq] + qam_sb.begin()))
        qam_sb.wrote(cq)
        aw_w = aw_sb.begin(); a_w = a_sb.begin()
        for tq in range(4):
            sl = slice(tq * 512, (tq + 1) * 512)
            pA = st.pb[4 + tq % 2]
            warA = pA.begin()
            mma = None
            for ct in range(NCT):
                mma = pA.wrote(kb.mm(pA.t[:], qgu.t[:, ct, :], h[ct].t[:, sl],
                                     start=(ct == 0), stop=False,
                                     deps=[l1] + h[ct].rdeps() + (warA if ct == 0 else [])))
                h[ct].read(mma); qgu.read(mma)
            mma = pA.wrote(kb.mm(pA.t[:], qam_sb.t[:], softf_buf.t[0:8, sl],
                                 start=False, stop=True, deps=[cq, l4]))
            softf_buf.read(mma); qam_sb.read(mma)
            ea = kb.act(a_sb.t[:, sl], pA.t[:], AF.Identity, deps=[mma] + a_w)
            pA.read(ea); a_sb.wrote(ea)
            pB = st.pb[4 + (tq + 1) % 2]
            mme = pB.wrote(kb.mm(pB.t[:], E8[:], softf_buf.t[0:8, sl],
                                 start=True, stop=True,
                                 deps=[E8_ld, l4] + pB.begin()))
            softf_buf.read(mme)
            wa = kb.v(lambda e, sl=sl, pB=pB, a_sb=a_sb, aw_sb=aw_sb: e.tensor_mul(
                aw_sb.t[:, sl], a_sb.t[:, sl], pB.t[:]), [ea, mme] + aw_w)
            pB.read(wa); aw_sb.wrote(wa); a_sb.read(wa)
        awu_cp = kb.dma("sync", aw_u.t[0:64, :], aw_sb.t[64:128, :],
                        deps=aw_sb.rdeps() + aw_u.begin())
        aw_u.wrote(awu_cp); aw_sb.read(awu_cp)
        gm_w = [gmt[m].begin() for m in range(2)]
        gm_ev = [None, None]
        guw = gu_f.begin()
        for itv in range(NIT):
            c0 = itv * 128
            wgt_ = wstr[0]; wut_ = wstr[1]
            lg = wgt_.wrote(kb.dma("gpsimd", wgt_.t[:],
                                   W["gwT"][f][:, c0:c0 + 128].rearrange("(a p) n -> p a n", p=128),
                                   deps=wgt_.begin()))
            lu = wut_.wrote(kb.dma("gpsimd", wut_.t[:],
                                   W["uwT"][f][:, c0:c0 + 128].rearrange("(a p) n -> p a n", p=128),
                                   deps=wut_.begin()))
            for m, (wt_, ld_) in enumerate(((wgt_, lg), (wut_, lu))):
                pGm = st.pb[4]
                warm = pGm.begin()
                mmg = None
                for ct in range(NCT):
                    mmg = pGm.wrote(kb.mm(pGm.t[0:8, 0:128], muw.t[:, ct, :],
                                          wt_.t[:, ct, :], start=(ct == 0),
                                          stop=(ct == NCT - 1),
                                          deps=[l3, ld_] + (warm if ct == 0 else [])))
                    wt_.read(mmg); muw.read(mmg)
                ev_ = pGm.read(kb.v(lambda e, m=m, c0=c0, pGm=pGm: e.tensor_copy(
                    gmt[m].t[0:8, c0:c0 + 128], pGm.t[0:8, 0:128]),
                    [mmg] + (gm_w[m] if itv == 0 else [])))
                gmt[m].wrote(ev_)
                gm_ev[m] = ev_
            for tq in range(4):
                sl = slice(tq * 512, (tq + 1) * 512)
                pG = st.pb[0 + tq % 2]; pU = st.pb[2 + tq % 2]
                warG = pG.begin(); warU = pU.begin()
                mmg2 = mmu2 = None
                for ct in range(NCT):
                    mmg2 = pG.wrote(kb.mm(pG.t[:], wgt_.t[:, ct, :], h[ct].t[:, sl],
                                          start=(ct == 0), stop=False,
                                          deps=[lg] + h[ct].rdeps() + (warG if ct == 0 else [])))
                    h[ct].read(mmg2); wgt_.read(mmg2)
                    mmu2 = pU.wrote(kb.mm(pU.t[:], wut_.t[:, ct, :], h[ct].t[:, sl],
                                          start=(ct == 0), stop=False,
                                          deps=[lu] + h[ct].rdeps() + (warU if ct == 0 else [])))
                    h[ct].read(mmu2); wut_.read(mmu2)
                for m, ps in ((0, pG), (1, pU)):
                    mmx = ps.wrote(kb.mm(ps.t[:], gmt[m].t[0:8, c0:c0 + 128],
                                         softf_buf.t[0:8, sl], start=False, stop=False,
                                         deps=[gm_ev[m], l4]))
                    softf_buf.read(mmx); gmt[m].read(mmx)
                    awsrc = aw_sb if m == 0 else aw_u
                    mmx2 = ps.wrote(kb.mm(ps.t[:],
                                          qbgu[m].t[0:64, c0:c0 + 128],
                                          awsrc.t[0:64, sl], start=False, stop=True,
                                          deps=psc_ops + awsrc.rdeps()))
                    awsrc.read(mmx2); qbgu[m].read(mmx2)
                    if m == 0:
                        mmg2 = mmx2
                    else:
                        mmu2 = mmx2
                pS2 = st.pb[5]
                mms = pS2.wrote(kb.mm(pS2.t[:], SEL[0:8, f * 128:(f + 1) * 128],
                                      scal8.t[0:8, sl], start=True, stop=True,
                                      deps=[SEL_ld] + scal8.rdeps() + pS2.begin()))
                scal8.read(mms)
                sce = kb.act(scbc[tq % 2].t[:], pS2.t[:], AF.Identity,
                             deps=[mms] + scbc[tq % 2].begin())
                pS2.read(sce); scbc[tq % 2].wrote(sce)
                ge = kb.act(gtmp[tq % 2].t[:], pG.t[:], AF.Silu,
                            deps=[mmg2] + gtmp[tq % 2].begin())
                pG.read(ge); gtmp[tq % 2].wrote(ge)
                ue = kb.act(utmp[tq % 2].t[:], pU.t[:], AF.Identity,
                            deps=[mmu2] + utmp[tq % 2].begin())
                pU.read(ue); utmp[tq % 2].wrote(ue)
                us = kb.op("gpsimd", lambda e, j3=tq % 2: e.tensor_mul(
                    utmp[j3].t[:], utmp[j3].t[:], scbc[j3].t[:]), [ue, sce])
                utmp[tq % 2].wrote(us); scbc[tq % 2].read(us)
                gm2 = kb.v(lambda e, itv=itv, sl=sl, j3=tq % 2: e.tensor_mul(
                    gu_f.t[:, itv, sl], gtmp[j3].t[:], utmp[j3].t[:]),
                    [ge, us] + (guw if itv == 0 and tq == 0 else []))
                gtmp[tq % 2].read(gm2); utmp[tq % 2].read(gm2); gu_f.wrote(gm2)
            r0 = ISH + f * NI + itv * 128
            sog = kb.dma("sync", W["gu_scr"][r0:r0 + 128, :],
                         gu_f.t[:, itv, :], deps=gu_f.rdeps())
            gu_f.read(sog)
            gu_stores.append(sog)
        sqd = kb.dma("sync", W["qbd_scr"][f], qbd_sb.t[:], deps=qbd_sb.rdeps())
        qbd_sb.read(sqd)
        qbd_stores.append(sqd)
    relbufs = ([gu_f] + h + wstr + gub + [qgu, gwp, qbd_sb, awd_buf, a_sb, aw_sb,
                softf_buf, scal8] + qbgu + gmt)
    ar.release(amark, relbufs)
    y = [ar.alloc([128, T]) for _ in range(NCT)]
    guc = ar.alloc([128, NIT, T])
    awd_buf = ar.alloc([64, T])
    a_sb = ar.alloc([128, T])
    softf_buf = ar.alloc([8, T])
    qad = ar.alloc([128, NIT, 64])
    dwt = [ar.alloc([128, NIT, 128]) for j in range(2)]
    qbdt = SB(sbuf(ctx, nc, f"{name}_qbdt", [64, 128], BF16))
    dtmp = [ar.alloc([128, 512]) for j in range(2)]
    xf = [st.arena32.alloc([128, 512]) for j in range(2)]
    rtmp = [st.arena32.alloc([128, 512]) for j in range(2)]
    for p in range(10):
        gw_ = guc.begin()
        g_lds = []
        if p < 2:
            gr0 = p * NIT * 128
            nblk = NIT
        else:
            f = p - 2
            gr0 = ISH + f * NI
            nblk = NIT
        for blk in range(nblk):
            o_ = kb.dma("sync", guc.t[:, blk, :],
                        W["gu_scr"][gr0 + blk * 128:gr0 + (blk + 1) * 128, :],
                        deps=(gw_ if blk == 0 else []) + [gu_stores[-1]])
            guc.wrote(o_)
            g_lds.append(o_)
        if p >= 2:
            f = p - 2
            l2 = qad.wrote(kb.dma("gpsimd", qad.t[:],
                                  W["qadT"][f].rearrange("(a p2) n -> p2 a n", p2=128),
                                  deps=qad.begin()))
            l4 = softf_buf.wrote(kb.dma("sync", softf_buf.t[:],
                                        W["softT_scr"][0:8, f * T:(f + 1) * T],
                                        deps=softf_buf.begin() + [softT_stores[f]]))
            aww = awd_buf.begin()
            for tq in range(4):
                sl = slice(tq * 512, (tq + 1) * 512)
                pA = st.pb[4 + tq % 2]
                warA = pA.begin()
                mma = None
                for blk in range(NIT):
                    mma = pA.wrote(kb.mm(pA.t[0:64, :], qad.t[:, blk, :],
                                         guc.t[:, blk, sl], start=(blk == 0),
                                         stop=(blk == NIT - 1),
                                         deps=[l2, g_lds[blk]] + (warA if blk == 0 else [])))
                    guc.read(mma); qad.read(mma)
                ea = kb.act(a_sb.t[0:64, sl], pA.t[0:64, :], AF.Identity,
                            deps=[mma] + (a_sb.begin() if tq == 0 else []))
                pA.read(ea); a_sb.wrote(ea)
                pB = st.pb[4 + (tq + 1) % 2]
                mme = pB.wrote(kb.mm(pB.t[0:64, :], E8[0:8, 0:64],
                                     softf_buf.t[0:8, sl], start=True, stop=True,
                                     deps=[E8_ld, l4] + pB.begin()))
                softf_buf.read(mme)
                wa = kb.v(lambda e, sl=sl, pB=pB, a_sb=a_sb, awd_buf=awd_buf: e.tensor_mul(
                    awd_buf.t[0:64, sl], a_sb.t[0:64, sl], pB.t[0:64, :]),
                    [ea, mme] + aww)
                pB.read(wa); awd_buf.wrote(wa); a_sb.read(wa)
        for ct in range(NCT):
            c0 = ct * 128
            jd = ct % 2
            dw_b = dwt[jd]
            if p < 2:
                dap = W["shdT"][p * NIT * 128:p * NIT * 128 + nblk * 128,
                                c0:c0 + 128].rearrange("(a p2) n -> p2 a n", p2=128)
            else:
                dap = W["dwT"][p - 2][:, c0:c0 + 128].rearrange("(a p2) n -> p2 a n", p2=128)
            ldd = dw_b.wrote(kb.dma("gpsimd", dw_b.t[:, 0:nblk, :], dap,
                                    deps=dw_b.begin()))
            if p >= 2:
                lqd = qbdt.wrote(kb.dma("sync", qbdt.t[:],
                                        W["qbd_scr"][p - 2, :, c0:c0 + 128],
                                        deps=qbdt.begin() + [qbd_stores[p - 2]]))
            for tq in range(4):
                sl = slice(tq * 512, (tq + 1) * 512)
                pbk = st.pb[tq % 2]
                war = pbk.begin()
                mml = None
                for blk in range(nblk):
                    mml = pbk.wrote(kb.mm(pbk.t[:], dw_b.t[:, blk, :], guc.t[:, blk, sl],
                                          start=(blk == 0),
                                          stop=(blk == nblk - 1 and p < 2),
                                          deps=[ldd, g_lds[blk]] + (war if blk == 0 else [])))
                    guc.read(mml); dw_b.read(mml)
                if p >= 2:
                    mml = pbk.wrote(kb.mm(pbk.t[:], qbdt.t[:],
                                          awd_buf.t[0:64, sl],
                                          start=False, stop=True,
                                          deps=[lqd] + awd_buf.rdeps()))
                    qbdt.read(mml); awd_buf.read(mml)
                if p == 0:
                    ev = kb.act(y[ct].t[:, sl], pbk.t[:], AF.Identity,
                                deps=[mml] + (y[ct].begin() if tq == 0 else []))
                    pbk.read(ev); y[ct].wrote(ev)
                else:
                    jj = tq % 2
                    ev = kb.act(dtmp[jj].t[:], pbk.t[:], AF.Identity,
                                deps=[mml] + dtmp[jj].begin())
                    pbk.read(ev); dtmp[jj].wrote(ev)
                    eng = "vector" if (ct + tq) % 2 == 0 else "gpsimd"
                    ac = kb.op(eng, lambda e, ct=ct, sl=sl, jj=jj: e.tensor_add(
                        y[ct].t[:, sl], y[ct].t[:, sl], dtmp[jj].t[:]),
                        [ev] + y[ct].rdeps())
                    y[ct].wrote(ac); dtmp[jj].read(ac)
    for ct in range(NCT):
        c0 = ct * 128
        for tq in range(4):
            sl = slice(tq * 512, (tq + 1) * 512)
            u = (ct * 4 + tq) % 2
            lx = xf[u].wrote(kb.dma("sync", xf[u].t[:], x_hbm[c0:c0 + 128, sl],
                                    deps=xf[u].begin()))
            ad = kb.v(lambda e, u=u, ct=ct, sl=sl: e.tensor_add(
                rtmp[u].t[:], y[ct].t[:, sl], xf[u].t[:]),
                [lx] + y[ct].rdeps() + rtmp[u].begin())
            rtmp[u].wrote(ad); xf[u].read(ad); y[ct].read(ad)
            so = kb.dma("sync", xout_hbm[c0:c0 + 128, sl], rtmp[u].t[:], deps=[ad])
            rtmp[u].read(so)
    ar.release(amark, y + [guc, awd_buf, a_sb])



def build_l1ffn_program(trunc=None):
    nc = bass.Bass("TRN2", target_bir_lowering=False, debug=False)
    xin = nc.dram_tensor("xin", [H, T], F32, kind="ExternalInput").ap()
    W = {}
    W["gwpT"] = nc.dram_tensor("gwpT", [H, 64], F32, kind="ExternalInput").ap()
    W["muw"] = nc.dram_tensor("muw", [8, H, 8], F32, kind="ExternalInput").ap()
    W["gwT"] = nc.dram_tensor("gwT", [8, H, NI], F32, kind="ExternalInput").ap()
    W["uwT"] = nc.dram_tensor("uwT", [8, H, NI], F32, kind="ExternalInput").ap()
    W["dwT"] = nc.dram_tensor("dwT", [8, NI, H], F32, kind="ExternalInput").ap()
    W["qaguT"] = nc.dram_tensor("qaguT", [8, H, 128], F32, kind="ExternalInput").ap()
    W["qadT"] = nc.dram_tensor("qadT", [8, NI, 64], F32, kind="ExternalInput").ap()
    W["qbg"] = nc.dram_tensor("qbg", [8, 64, NI], F32, kind="ExternalInput").ap()
    W["qbu"] = nc.dram_tensor("qbu", [8, 64, NI], F32, kind="ExternalInput").ap()
    W["qbd"] = nc.dram_tensor("qbd", [8, 64, H], F32, kind="ExternalInput").ap()
    W["gsc"] = nc.dram_tensor("gsc", [8, NI], F32, kind="ExternalInput").ap()
    W["usc"] = nc.dram_tensor("usc", [8, NI], F32, kind="ExternalInput").ap()
    W["dsc"] = nc.dram_tensor("dsc", [8, H], F32, kind="ExternalInput").ap()
    W["shgT"] = nc.dram_tensor("shgT", [H, ISH], F32, kind="ExternalInput").ap()
    W["shuT"] = nc.dram_tensor("shuT", [H, ISH], F32, kind="ExternalInput").ap()
    W["shdT"] = nc.dram_tensor("shdT", [ISH, H], F32, kind="ExternalInput").ap()
    W["gu_scr"] = nc.dram_tensor("gu_scr", [ISH + 8 * NI, T], BF16, kind="Internal").ap()
    W["softT_scr"] = nc.dram_tensor("softT_scr", [8, 8 * T], BF16, kind="Internal").ap()
    W["qbd_scr"] = nc.dram_tensor("qbd_scr", [8, 64, H], BF16, kind="Internal").ap()
    xout = nc.dram_tensor("xout", [H, T], F32, kind="ExternalOutput").ap()
    ctx = ExitStack()
    with ctx:
        kb = KB(nc)
        st = St(nc, kb, ctx, arena_words=85504, arena32_words=6272)
        emit_l1ffn(st, xin, xout, W)
        if trunc is not None:
            kb.ops = kb.ops[:trunc]
        with nc.Block() as block:
            kb.emit(block, ctx)
    return nc

def stage_l1ffn(inp):
    c = np.ascontiguousarray
    inv = np.asarray(inp["L1_ffn_inv_mapping"]).reshape(-1)
    d = {}
    d["gwpT"] = c(inp["L1_ffn_gate_weight"][inv].T)
    d["muw"] = c(inp["L1_ffn_mask_up_W"])
    d["gwT"] = c(inp["L1_ffn_gate_W"].transpose(0, 2, 1))
    d["uwT"] = c(inp["L1_ffn_up_W"].transpose(0, 2, 1))
    d["dwT"] = c(inp["L1_ffn_down_W"].transpose(0, 2, 1))
    d["qaguT"] = c(np.concatenate([
        inp["L1_ffn_gate_qa"].reshape(8, 64, H),
        inp["L1_ffn_up_qa"].reshape(8, 64, H)], axis=1).transpose(0, 2, 1))
    d["qadT"] = c(inp["L1_ffn_down_qa"].reshape(8, 64, NI).transpose(0, 2, 1))
    d["qbg"] = c(inp["L1_ffn_gate_qb"].transpose(0, 1, 3, 2).reshape(8, 64, NI))
    d["qbu"] = c(inp["L1_ffn_up_qb"].transpose(0, 1, 3, 2).reshape(8, 64, NI))
    d["qbd"] = c(inp["L1_ffn_down_qb"].transpose(0, 1, 3, 2).reshape(8, 64, H))
    d["gsc"] = c(inp["L1_ffn_gate_scale"])
    d["usc"] = c(inp["L1_ffn_up_scale"])
    d["dsc"] = c(inp["L1_ffn_down_scale"])
    d["shgT"] = c(inp["L1_ffn_sh_gate_W"].T)
    d["shuT"] = c(inp["L1_ffn_sh_up_W"].T)
    d["shdT"] = c(inp["L1_ffn_sh_down_W"].T)
    return d


def emit_l1gla(st, x_hbm, xout_hbm, W, name="gla"):
    nc, kb, ctx = st.nc, st.kb, st.ctx
    ar = st.arena
    NH, HD, CH = 16, 128, 128
    NCH = T // CH
    onesr, onesr_ld = st.ones_row()
    onesc, onesc_ld = st.ones_col()
    identb, ident_ld = st.ident()
    idf32, idf32_ld = st.const("c_identf32", np.eye(128, dtype=np.float32), F32)
    onesr32, onesr32_ld = st.const("c_onesr32", np.ones((1, 128), np.float32), F32)
    sel = np.zeros((16, 16 * 128), np.float32)
    for hh in range(16):
        sel[hh, hh * 128:(hh + 1) * 128] = 1.0
    SELF, SELF_ld = st.const("c_sel16f", sel, F32)
    half = 64
    inv_freq = 1.0 / (10000.0 ** (np.arange(half, dtype=np.float32) / half))
    fr = np.arange(T, dtype=np.float32)[:, None] * inv_freq[None, :]
    cosm = np.cos(fr).T; sinm = np.sin(fr).T
    cfull = np.concatenate([cosm, cosm], 0)
    ssign = np.concatenate([-sinm, sinm], 0)
    CF, CF_ld = st.const("c_ropec", cfull)
    SF, SF_ld = st.const("c_ropes", ssign)
    a_ = np.arange(128)[:, None]; b_ = np.arange(128)[None, :]
    gmask, gmask_ld = st.const("c_gmask", np.where(a_ <= b_, 0.0, -1e9).astype(np.float32), F32)
    amark = ar.mark()
    h = [ar.alloc([128, T]) for _ in range(NCT)]
    emit_ln(st, x_hbm, name=name + "_ln", h_tiles=h)
    ohead = [SB(sbuf(ctx, nc, f"{name}_ohd{j}", [128, T], BF16)) for j in range(2)]
    wg_sb = SB(sbuf(ctx, nc, f"{name}_wg", [128, NCT, 16], BF16))
    wg_sb.wrote(kb.dma("gpsimd", wg_sb.t[:], W["wg"].rearrange("(a p) n -> p a n", p=128),
                       deps=wg_sb.begin()))
    bg_sb = SB(sbuf(ctx, nc, f"{name}_bg", [16, 1], F32))
    bg_sb.wrote(kb.dma("sync", bg_sb.t[:], W["bg"], deps=bg_sb.begin()))
    rows32 = st.arena32
    r32m = rows32.mark()
    cum = rows32.alloc([16, T])
    e_T = rows32.alloc([16, T])
    g_T = rows32.alloc([16, T])
    ncol = SB(sbuf(ctx, nc, f"{name}_ncol", [16, 1], F32))
    onescan, onescan_ld = st.const("c_ones16c", np.ones((16, 1), np.float32), BF16)
    beta_ops = []
    for tq in range(4):
        sl = slice(tq * 512, (tq + 1) * 512)
        pbk = st.pb[tq % 2]
        war = pbk.begin()
        mml = None
        for ct in range(NCT):
            mml = pbk.wrote(kb.mm(pbk.t[0:16, :], wg_sb.t[:, ct, :], h[ct].t[:, sl],
                                  start=(ct == 0), stop=(ct == NCT - 1),
                                  deps=wg_sb.rdeps() + h[ct].rdeps() +
                                  (war if ct == 0 else [])))
            h[ct].read(mml); wg_sb.read(mml)
        ev = kb.act(cum.t[0:16, sl], pbk.t[0:16, :], AF.Sigmoid, bias=bg_sb.t[:],
                    deps=[mml] + bg_sb.rdeps() + (cum.begin() if tq == 0 else []))
        pbk.read(ev); cum.wrote(ev); bg_sb.read(ev)
        o1 = kb.v(lambda e, sl=sl: e.tensor_scalar(
            cum.t[0:16, sl], cum.t[0:16, sl], 0.8, 0.9995, OP.max, OP.min), [ev])
        cum.wrote(o1)
        o2 = kb.act(cum.t[0:16, sl], cum.t[0:16, sl], AF.Ln, deps=[o1])
        cum.wrote(o2)
        beta_ops.append(o2)
    sc1 = kb.v(lambda e: e.tensor_tensor_scan(
        cum.t[0:16, :], onescan[:].broadcast_to([16, T]),
        cum.t[0:16, :], 0.0, OP.mult, OP.add),
        beta_ops + [onescan_ld])
    cum.wrote(sc1)

    zcol, zcol_ld = st.const("c_zero16", np.zeros((16, 1), np.float32), F32)
    eg_ops = []
    ew = e_T.begin(); gw = g_T.begin()
    for c in range(NCH):
        c0 = c * CH
        if c > 0:
            nb = kb.v(lambda e, c0=c0: e.tensor_scalar_mul(
                ncol.t[:], cum.t[0:16, c0 - 1:c0], -1.0), [sc1] + ncol.begin())
            ncol.wrote(nb)
            bias_ap = ncol.t[:]
            bdep = [(nb, False)]
        else:
            bias_ap = zcol[:]
            bdep = [zcol_ld]
        o1 = kb.act(e_T.t[0:16, c0:c0 + CH], cum.t[0:16, c0:c0 + CH], AF.Exp,
                    bias=bias_ap, deps=[sc1] + bdep + ew)
        e_T.wrote(o1); ncol.read(o1)
        o2 = kb.act(g_T.t[0:16, c0:c0 + CH], cum.t[0:16, c0:c0 + CH], AF.Exp,
                    scale=-1.0, bias=cum.t[0:16, c0 + CH - 1:c0 + CH],
                    deps=[sc1] + gw)
        g_T.wrote(o2)
        eg_ops += [o1, o2]
    alpha = SB(sbuf(ctx, nc, f"{name}_alpha", [16, NCH], F32))
    al1 = kb.v(lambda e: e.tensor_copy(
        alpha.t[:], e_T.t[0:16, CH - 1:T:CH].rearrange("p a -> p a")),
        e_T.rdeps() + alpha.begin())
    alpha.wrote(al1)
    cumtok = rows32.alloc([128, NCH * 16])
    gtok = rows32.alloc([128, NCH * 16])
    ctw = cumtok.begin(); gtw = gtok.begin()
    for c in range(NCH):
        c0 = c * CH
        pT = st.pb[2]
        t1 = pT.wrote(kb.trans(pT.t[0:128, 0:16],
                               cum.t[0:16, c0:c0 + CH], idf32[0:16, 0:16],
                               deps=[sc1, idf32_ld] + pT.begin()))
        cum.read(t1)
        cp1 = pT.read(kb.v(lambda e, c=c, pT=pT: e.tensor_copy(
            cumtok.t[:, c * 16:(c + 1) * 16], pT.t[0:128, 0:16]), [t1] + ctw))
        cumtok.wrote(cp1)
        t2 = pT.wrote(kb.trans(pT.t[0:128, 0:16], g_T.t[0:16, c0:c0 + CH],
                               idf32[0:16, 0:16],
                               deps=eg_ops + [idf32_ld] + pT.begin()))
        g_T.read(t2)
        cp2 = pT.read(kb.v(lambda e, c=c, pT=pT: e.tensor_copy(
            gtok.t[:, c * 16:(c + 1) * 16], pT.t[0:128, 0:16]), [t2] + gtw))
        gtok.wrote(cp2)
    oscr_stores = []
    wsl = [SB(sbuf(ctx, nc, f"{name}_w{m}", [128, NCT, 128], BF16)) for m in range(3)]
    qkv = [SB(sbuf(ctx, nc, f"{name}_qkv{m}", [128, T], BF16)) for m in range(3)]
    qsw = SB(sbuf(ctx, nc, f"{name}_qsw", [128, T], BF16))
    tmp1 = SB(sbuf(ctx, nc, f"{name}_tmp1", [128, T], BF16))
    ktok = SB(sbuf(ctx, nc, f"{name}_ktok", [128, NCH, 128], BF16))
    vtok = SB(sbuf(ctx, nc, f"{name}_vtok", [128, NCH, 128], BF16))
    S_sb = SB(sbuf(ctx, nc, f"{name}_S", [128, 128], BF16))
    z_sb = SB(sbuf(ctx, nc, f"{name}_z", [128, 1], BF16))
    S32 = SB(sbuf(ctx, nc, f"{name}_S32", [128, 128], F32))
    z32 = SB(sbuf(ctx, nc, f"{name}_z32", [128, 1], F32))
    gam = [SB(sbuf(ctx, nc, f"{name}_gam{j}", [128, 128], BF16)) for j in range(2)]
    w2 = [SB(sbuf(ctx, nc, f"{name}_w2{j}", [128, 128], BF16)) for j in range(2)]
    qsc = SB(sbuf(ctx, nc, f"{name}_qsc", [128, 128], BF16))
    kdec = SB(sbuf(ctx, nc, f"{name}_kdec", [128, 128], BF16))
    dres = Rows(ctx, nc, name + "_dr", CH, n_max=1)
    r_den = dres.row("den")
    acol = SB(sbuf(ctx, nc, f"{name}_acol", [128, 1], F32))
    otmp = SB(sbuf(ctx, nc, f"{name}_otmp", [128, 128], BF16))
    dtok = SB(sbuf(ctx, nc, f"{name}_dtok", [128, 16], F32))
    for hh in range(NH):
        lds = []
        for m, wkey in enumerate(("wq", "wk", "wv")):
            wb = wsl[m]
            o = wb.wrote(kb.dma("gpsimd", wb.t[:],
                                W[wkey][:, hh * 128:(hh + 1) * 128].rearrange("(a p) n -> p a n", p=128),
                                deps=wb.begin()))
            lds.append(o)
        for m in range(3):
            dst = qkv[m]
            dw = dst.begin()
            for tq in range(4):
                pbk = st.pb[tq % 2]
                sl = slice(tq * 512, (tq + 1) * 512)
                war = pbk.begin()
                mml = None
                for ct in range(NCT):
                    mml = pbk.wrote(kb.mm(pbk.t[:], wsl[m].t[:, ct, :], h[ct].t[:, sl],
                                          start=(ct == 0), stop=(ct == NCT - 1),
                                          deps=[lds[m]] + h[ct].rdeps() +
                                          (war if ct == 0 else [])))
                    h[ct].read(mml); wsl[m].read(mml)
                ev = kb.act(dst.t[:, sl], pbk.t[:], AF.Identity, deps=[mml] + dw)
                pbk.read(ev); dst.wrote(ev)
        for m in range(2):
            src_b = qkv[m]
            s1 = kb.dma("sync", qsw.t[0:64, :], src_b.t[64:128, :],
                        deps=src_b.rdeps() + qsw.begin())
            s2 = kb.dma("sync", qsw.t[64:128, :], src_b.t[0:64, :],
                        deps=src_b.rdeps())
            qsw.wrote(s1); qsw.wrote(s2); src_b.read(s1); src_b.read(s2)
            p1 = kb.v(lambda e, src_b=src_b: e.tensor_mul(tmp1.t[:], src_b.t[:], CF[:]),
                      src_b.rdeps() + [CF_ld] + tmp1.begin())
            tmp1.wrote(p1); src_b.read(p1)
            p2 = kb.v(lambda e, src_b=src_b: e.scalar_tensor_tensor(
                src_b.t[:], qsw.t[:], 1.0, SF[:], OP.mult, OP.bypass),
                [s1, s2, SF_ld] + src_b.begin()) if False else None
            p2 = kb.v(lambda e: e.tensor_mul(qsw.t[:], qsw.t[:], SF[:]),
                      [s1, s2, SF_ld])
            qsw.wrote(p2)
            p3 = kb.v(lambda e, src_b=src_b: e.tensor_add(src_b.t[:], tmp1.t[:], qsw.t[:]),
                      [p1, p2] + src_b.begin())
            src_b.wrote(p3); tmp1.read(p3); qsw.read(p3)
            f1 = kb.v(lambda e, src_b=src_b: e.tensor_scalar_min(tmp1.t[:], src_b.t[:], 0.0),
                      [p3] + tmp1.begin())
            tmp1.wrote(f1); src_b.read(f1)
            f2 = kb.act(tmp1.t[:], tmp1.t[:], AF.Exp, deps=[f1])
            tmp1.wrote(f2)
            f3 = kb.v(lambda e, src_b=src_b: e.tensor_relu(qsw.t[:], src_b.t[:]),
                      [p3] + qsw.begin())
            qsw.wrote(f3); src_b.read(f3)
            f4 = kb.v(lambda e, src_b=src_b: e.tensor_add(src_b.t[:], tmp1.t[:], qsw.t[:]),
                      [f2, f3] + src_b.begin())
            src_b.wrote(f4); tmp1.read(f4); qsw.read(f4)
        qs, ks, vs = qkv
        for (src_b, dst_b) in ((ks, ktok), (vs, vtok)):
            dw = dst_b.begin()
            for c in range(NCH):
                pb = st.pbf[c % 2]
                tr = pb.wrote(kb.trans(pb.t[:, 0:128], src_b.t[:, c * CH:(c + 1) * CH],
                                       identb[:], deps=src_b.rdeps() + [ident_ld] + pb.begin()))
                src_b.read(tr)
                cp = pb.read(kb.v(lambda e, c=c, pb=pb, dst_b=dst_b: e.tensor_copy(
                    dst_b.t[:, c, :], pb.t[:, 0:128]), [tr] + dw))
                dst_b.wrote(cp)
        z0 = kb.v(lambda e: e.memset(z_sb.t[:], 0.0), z_sb.begin())
        z_sb.wrote(z0)
        s0 = kb.v(lambda e: e.memset(S_sb.t[:], 0.0), S_sb.begin())
        S_sb.wrote(s0)
        z0b = kb.v(lambda e: e.memset(z32.t[:], 0.0), z32.begin())
        z32.wrote(z0b)
        s0b = kb.v(lambda e: e.memset(S32.t[:], 0.0), S32.begin())
        S32.wrote(s0b)
        ow = ohead[hh % 2].begin()
        for c in range(NCH):
            c0 = c * CH
            slc = slice(c0, c0 + CH)
            jg = c % 2
            pD = st.pb[2]
            mmd = pD.wrote(kb.mm(pD.t[:, 0:128], SELF[0:16, hh * 128:(hh + 1) * 128],
                                 cum.t[0:16, slc], start=True, stop=False,
                                 deps=[SELF_ld, sc1] + pD.begin()))
            cum.read(mmd)
            mmd2 = pD.wrote(kb.mm(pD.t[:, 0:128], idf32[:], gmask[:],
                                  start=False, stop=True, deps=[gmask_ld, idf32_ld]))
            ds = kb.v(lambda e, c=c, hh=hh, pD=pD: e.tensor_scalar(
                pD.t[:, 0:128], pD.t[:, 0:128],
                cumtok.t[:, c * 16 + hh:c * 16 + hh + 1], None,
                OP.subtract), cumtok.rdeps() + [mmd2])
            pD.wrote(ds); cumtok.read(ds)
            ex = kb.act(gam[jg].t[:], pD.t[:, 0:128], AF.Exp,
                        deps=[ds] + gam[jg].begin())
            pD.read(ex); gam[jg].wrote(ex)
            pS = st.pb[3]
            mms = pS.wrote(kb.mm(pS.t[:, 0:128], ks.t[:, slc], qs.t[:, slc],
                                 start=True, stop=True,
                                 deps=ks.rdeps() + qs.rdeps() + pS.begin()))
            ks.read(mms); qs.read(mms)
            w2o = kb.v(lambda e, jg=jg, pS=pS: e.tensor_mul(
                w2[jg].t[:], pS.t[:, 0:128], gam[jg].t[:]),
                [mms, ex] + w2[jg].begin())
            pS.read(w2o); gam[jg].read(w2o); w2[jg].wrote(w2o)
            pE = st.pb[2]
            mme = pE.wrote(kb.mm(pE.t[:, 0:128], SELF[0:16, hh * 128:(hh + 1) * 128],
                                 e_T.t[0:16, slc], start=True, stop=True,
                                 deps=[SELF_ld] + e_T.rdeps() + pE.begin()))
            e_T.read(mme)
            qso = kb.v(lambda e, pE=pE: e.tensor_mul(qsc.t[:], qs.t[:, slc], pE.t[:, 0:128]),
                       [mme] + qs.rdeps() + qsc.begin())
            pE.read(qso); qs.read(qso); qsc.wrote(qso)
            pO = st.pb[4]
            warO = pO.begin()
            mo1 = pO.wrote(kb.mm(pO.t[:, 0:128], vtok.t[:, c, :], w2[jg].t[:],
                                 start=True, stop=False,
                                 deps=vtok.rdeps() + [w2o] + warO))
            vtok.read(mo1); w2[jg].read(mo1)
            mo2 = pO.wrote(kb.mm(pO.t[:, 0:128], S_sb.t[:], qsc.t[:],
                                 start=False, stop=True,
                                 deps=S_sb.rdeps() + [qso]))
            S_sb.read(mo2); qsc.read(mo2)
            pN = st.pb[5]
            warN = pN.begin()
            mn1 = pN.wrote(kb.mm(pN.t[0:1, 0:128], onesc[:, 0:1], w2[jg].t[:],
                                 start=True, stop=False,
                                 deps=[onesc_ld, w2o] + warN))
            w2[jg].read(mn1)
            mn2 = pN.wrote(kb.mm(pN.t[0:1, 0:128], z_sb.t[:], qsc.t[:],
                                 start=False, stop=True,
                                 deps=z_sb.rdeps() + [qso]))
            z_sb.read(mn2); qsc.read(mn2)
            d1 = pN.read(kb.v(lambda e, pN=pN: e.tensor_scalar_add(
                r_den.t[:], pN.t[0:1, 0:128], 1e-6), [mn2] + r_den.begin()))
            r_den.wrote(d1)
            d2 = kb.v(lambda e: e.reciprocal(r_den.t[:], r_den.t[:]), [d1])
            r_den.wrote(d2)
            pR = st.pb[5]
            mmr = pR.wrote(kb.mm(pR.t[:, 0:128], onesr32[:], r_den.t[:],
                                 start=True, stop=True,
                                 deps=[onesr32_ld, d2] + pR.begin()))
            r_den.read(mmr)
            oe = kb.act(otmp.t[:], pO.t[:, 0:128], AF.Identity,
                        deps=[mo2] + otmp.begin())
            pO.read(oe); otmp.wrote(oe)
            no = kb.v(lambda e, hh=hh, slc=slc, pR=pR: e.tensor_mul(
                ohead[hh % 2].t[:, slc], otmp.t[:], pR.t[:, 0:128]),
                [oe, mmr] + ow)
            pR.read(no); otmp.read(no); ohead[hh % 2].wrote(no)
            kd = kb.v(lambda e, c=c, hh=hh: e.tensor_scalar_mul(
                kdec.t[:], ktok.t[:, c, :], gtok.t[:, c * 16 + hh:c * 16 + hh + 1]),
                ktok.rdeps() + gtok.rdeps() + kdec.begin())
            ktok.read(kd); gtok.read(kd); kdec.wrote(kd)
            pDS = st.pb[3]
            mds = pDS.wrote(kb.mm(pDS.t[:, 0:128], kdec.t[:], vtok.t[:, c, :],
                                  start=True, stop=True,
                                  deps=[kd] + vtok.rdeps() + pDS.begin()))
            kdec.read(mds); vtok.read(mds)
            pA2 = st.pb[2]
            mal = pA2.wrote(kb.mm(pA2.t[:, 0:1], SELF[0:16, hh * 128:(hh + 1) * 128],
                                  alpha.t[0:16, c:c + 1], start=True, stop=True,
                                  deps=[SELF_ld] + alpha.rdeps() + pA2.begin()))
            alpha.read(mal)
            ac = pA2.read(kb.v(lambda e, pA2=pA2: e.tensor_copy(acol.t[:], pA2.t[:, 0:1]),
                               [mal] + acol.begin()))
            acol.wrote(ac)
            su = kb.v(lambda e, pDS=pDS: e.scalar_tensor_tensor(
                S32.t[:], S32.t[:], acol.t[:], pDS.t[:, 0:128], OP.mult, OP.add),
                [mds, (ac, True)] + S32.rdeps())
            S32.wrote(su); pDS.read(su); acol.read(su)
            su2 = kb.v(lambda e: e.tensor_copy(S_sb.t[:], S32.t[:]),
                       [su] + S_sb.begin())
            S_sb.wrote(su2)
            pz = st.pb[3]
            mdz = pz.wrote(kb.mm(pz.t[:, 0:1], kdec.t[:], onesc[:, 0:1],
                                 start=True, stop=True,
                                 deps=[kd, onesc_ld] + pz.begin()))
            kdec.read(mdz)
            zu = kb.v(lambda e, pz=pz: e.scalar_tensor_tensor(
                z32.t[:], z32.t[:], acol.t[:], pz.t[:, 0:1], OP.mult, OP.add),
                [mdz, (ac, True)] + z32.rdeps())
            z32.wrote(zu); pz.read(zu); acol.read(zu)
            zu2 = kb.v(lambda e: e.tensor_copy(z_sb.t[:], z32.t[:]),
                       [zu] + z_sb.begin())
            z_sb.wrote(zu2)
        so = kb.dma("sync", W["oscr"][hh * 128:(hh + 1) * 128, :],
                    ohead[hh % 2].t[:], deps=ohead[hh % 2].rdeps())
        ohead[hh % 2].read(so)
        oscr_stores.append(so)
    ar.release(amark, h)
    o_t = [ar.alloc([128, T]) for _ in range(NH)]
    o_lds = []
    for ki in range(NH):
        o = o_t[ki].wrote(kb.dma("sync", o_t[ki].t[:],
                                 W["oscr"][ki * 128:(ki + 1) * 128, :],
                                 deps=o_t[ki].begin() + [oscr_stores[ki]]))
        o_lds.append(o)
    wout = [SB(sbuf(ctx, nc, f"{name}_wo{j}", [128, NCT, 128], BF16)) for j in range(1)]
    bo_t = [SB(sbuf(ctx, nc, f"{name}_bo{j}", [128, 1], F32)) for j in range(2)]
    xf = [st.arena32.alloc([128, 512]) for j in range(2)]
    rtmp = [st.arena32.alloc([128, 512]) for j in range(2)]
    for co in range(NCT):
        j = co % 2
        wo = wout[0]
        ldw = wo.wrote(kb.dma("gpsimd", wo.t[:],
                              W["wo"][:, co * 128:(co + 1) * 128].rearrange("(a p) n -> p a n", p=128),
                              deps=wo.begin()))
        ldb = bo_t[j].wrote(kb.dma("sync", bo_t[j].t[:],
                                   W["bo"][co * 128:(co + 1) * 128, 0:1],
                                   deps=bo_t[j].begin()))
        for tqc in range(4):
            sl = slice(tqc * 512, (tqc + 1) * 512)
            pbk = st.pb[tqc % 2]
            war = pbk.begin()
            mml = None
            for ki in range(NH):
                mml = pbk.wrote(kb.mm(pbk.t[:], wo.t[:, ki, :], o_t[ki].t[:, sl],
                                      start=(ki == 0), stop=(ki == NH - 1),
                                      deps=[ldw] + o_t[ki].rdeps() + (war if ki == 0 else [])))
                o_t[ki].read(mml); wo.read(mml)
            u = (co * 4 + tqc) % 2
            ev = kb.act(rtmp[u].t[:], pbk.t[:], AF.Identity, bias=bo_t[j].t[:],
                        deps=[mml, ldb] + rtmp[u].begin())
            pbk.read(ev); rtmp[u].wrote(ev); bo_t[j].read(ev)
            lx = xf[u].wrote(kb.dma("sync", xf[u].t[:], x_hbm[co * 128:(co + 1) * 128, sl],
                                    deps=xf[u].begin()))
            ad = kb.v(lambda e, u=u: e.tensor_add(rtmp[u].t[:], rtmp[u].t[:], xf[u].t[:]),
                      [ev, lx])
            rtmp[u].wrote(ad); xf[u].read(ad)
            so = kb.dma("sync", xout_hbm[co * 128:(co + 1) * 128, sl], rtmp[u].t[:],
                        deps=[ad])
            rtmp[u].read(so)
    ar.release(amark, o_t)
    rows32.release(r32m, [cum, e_T, g_T, cumtok, gtok] + xf + rtmp)

def build_l1gla_program(trunc=None):
    nc = bass.Bass("TRN2", target_bir_lowering=False, debug=False)
    xin = nc.dram_tensor("xin", [H, T], F32, kind="ExternalInput").ap()
    W = {}
    for nm in ("wq", "wk", "wv", "wo"):
        W[nm] = nc.dram_tensor(nm, [H, H], F32, kind="ExternalInput").ap()
    W["wg"] = nc.dram_tensor("wg", [H, 16], F32, kind="ExternalInput").ap()
    W["bg"] = nc.dram_tensor("bg", [16, 1], F32, kind="ExternalInput").ap()
    W["bo"] = nc.dram_tensor("bo", [H, 1], F32, kind="ExternalInput").ap()
    W["oscr"] = nc.dram_tensor("gla_oscr", [H, T], BF16, kind="Internal").ap()
    xout = nc.dram_tensor("xout", [H, T], F32, kind="ExternalOutput").ap()
    ctx = ExitStack()
    with ctx:
        kb = KB(nc)
        st = St(nc, kb, ctx, arena_words=46080, arena32_words=8960)
        emit_l1gla(st, xin, xout, W)
        if trunc is not None:
            kb.ops = kb.ops[:trunc]
        with nc.Block() as block:
            kb.emit(block, ctx)
    return nc

SSD_CH = 256
SSD_NCH = T // SSD_CH

def emit_l1ssd(st, x_hbm, xout_hbm, W, name="ssd"):
    nc, kb, ctx = st.nc, st.kb, st.ctx
    ar = st.arena
    a32 = st.arena32
    onesr, onesr_ld = st.ones_row()
    onesc, onesc_ld = st.ones_col()
    identb, ident_ld = st.ident()
    one64, one64_ld = st.const("c_one64", np.ones((64, 1), np.float32), F32)
    zc128, zc128_ld = st.const("c_zc128", np.zeros((128, 1), np.float32), F32)
    a_ = np.arange(128)[:, None]; c_ = np.arange(384)[None, :]
    emask, emask_ld = st.const("c_ssdmask",
                               np.where(a_ <= c_ - 128, 0.0, -1e9).astype(np.float32), F32)
    onesf, onesf_ld = st.const("c_onesf64", np.ones((64, 128), np.float32), F32)
    idf32, idf32_ld = st.const("c_idf64", np.eye(64, dtype=np.float32), F32)
    BT = ar.alloc([128, T])
    CTb = ar.alloc([128, T])
    acs = a32.alloc([64, T])
    e_T = a32.alloc([64, T])
    g_T = a32.alloc([64, T])
    acstok = a32.alloc([128, SSD_NCH * 2 * 64])
    gtok = a32.alloc([128, SSD_NCH * 2 * 64])
    dttok = a32.alloc([128, SSD_NCH * 2 * 64])
    alpha = SB(sbuf(ctx, nc, f"{name}_alpha", [64, SSD_NCH], F32))
    aco = SB(sbuf(ctx, nc, f"{name}_aco", [128, 1], F32))
    selh = SB(sbuf(ctx, nc, f"{name}_selh", [64, 128], F32))
    nbn = SB(sbuf(ctx, nc, f"{name}_nbn", [128, 1], F32))
    ncol = SB(sbuf(ctx, nc, f"{name}_ncol", [64, 1], F32))
    dtb_sb = SB(sbuf(ctx, nc, f"{name}_dtb", [64, 1], F32))
    alog_sb = SB(sbuf(ctx, nc, f"{name}_alog", [64, 1], F32))
    hmark = ar.mark()
    h = [ar.alloc([128, T]) for _ in range(NCT)]
    emit_ln(st, x_hbm, name=name + "_ln", h_tiles=h)
    m32a = a32.mark()
    dtr = a32.alloc([64, T])
    wcol = [SB(sbuf(ctx, nc, f"{name}_wc{j}", [128, NCT, 128], BF16)) for j in range(2)]
    ptile = [SB(sbuf(ctx, nc, f"{name}_ptl{j}", [128, T + 4], BF16)) for j in range(2)]
    ctile = [SB(sbuf(ctx, nc, f"{name}_ctl{j}", [128, T], BF16)) for j in range(2)]
    cvw = [SB(sbuf(ctx, nc, f"{name}_cvw{j}", [128, 4], F32)) for j in range(2)]
    cvb = [SB(sbuf(ctx, nc, f"{name}_cvb{j}", [128, 1], F32)) for j in range(2)]
    x_stores = {}
    z_stores = {}
    ldb = dtb_sb.wrote(kb.dma("sync", dtb_sb.t[:], W["dtb"], deps=dtb_sb.begin()))
    for ot in range(67):
        j = ot % 2
        wsel = wcol[j]
        if ot < 66:
            c0 = ot * 128
            ldw = wsel.wrote(kb.dma("gpsimd", wsel.t[:],
                                    W["win"][:, c0:c0 + 128].rearrange("(a p) n -> p a n", p=128),
                                    deps=wsel.begin()))
            is_z = ot < 32
            xbc0 = c0 - 4096
            dst = ctile[j] if is_z else ptile[j]
            dw = dst.begin()
            if not is_z:
                z0 = kb.v(lambda e, j=j: e.memset(ptile[j].t[:, 0:4], 0.0), dw)
                dst.wrote(z0)
            for tq in range(4):
                pbk = st.pb[tq % 2]
                sl = slice(tq * 512, (tq + 1) * 512)
                war = pbk.begin()
                mml = None
                for ct in range(NCT):
                    mml = pbk.wrote(kb.mm(pbk.t[:], wsel.t[:, ct, :], h[ct].t[:, sl],
                                          start=(ct == 0), stop=(ct == NCT - 1),
                                          deps=[ldw] + h[ct].rdeps() + (war if ct == 0 else [])))
                    h[ct].read(mml); wsel.read(mml)
                if is_z:
                    ev = kb.act(dst.t[:, sl], pbk.t[:], AF.Identity, deps=[mml] + dw)
                else:
                    ev = kb.act(dst.t[:, 4 + tq * 512:4 + (tq + 1) * 512], pbk.t[:],
                                AF.Identity, deps=[mml] + dw)
                pbk.read(ev); dst.wrote(ev)
            if is_z:
                so = kb.dma("sync", W["z_scr"][c0:c0 + 128, :], dst.t[:], deps=dst.rdeps())
                dst.read(so)
                z_stores[ot] = so
            else:
                lw = cvw[j].wrote(kb.dma("sync", cvw[j].t[:],
                                         W["convwT"][xbc0:xbc0 + 128, :], deps=cvw[j].begin()))
                lb = cvb[j].wrote(kb.dma("sync", cvb[j].t[:],
                                         W["convb"][xbc0:xbc0 + 128, 0:1], deps=cvb[j].begin()))
                cw = ctile[j].begin()
                o1 = kb.v(lambda e, j=j: e.tensor_scalar_mul(
                    ctile[j].t[:], ptile[j].t[:, 1:1 + T], cvw[j].t[:, 0:1]),
                    dst.rdeps() + [(lw, True)] + cw)
                ctile[j].wrote(o1); dst.read(o1); cvw[j].read(o1)
                for i in range(1, 4):
                    o1 = kb.v(lambda e, j=j, i=i: e.scalar_tensor_tensor(
                        ctile[j].t[:], ptile[j].t[:, 1 + i:1 + i + T], cvw[j].t[:, i:i + 1],
                        ctile[j].t[:], OP.mult, OP.add), [o1])
                    ctile[j].wrote(o1); dst.read(o1); cvw[j].read(o1)
                sil = kb.act(ctile[j].t[:], ctile[j].t[:], AF.Silu, bias=cvb[j].t[:],
                             deps=[o1, lb])
                ctile[j].wrote(sil); cvb[j].read(sil)
                if ot < 64:
                    so = kb.dma("sync", W["x_scr"][xbc0:xbc0 + 128, :], ctile[j].t[:],
                                deps=ctile[j].rdeps())
                    ctile[j].read(so)
                    x_stores[ot - 32] = so
                else:
                    tgt = BT if ot == 64 else CTb
                    cp = kb.v(lambda e, j=j, tgt=tgt: e.tensor_copy(tgt.t[:], ctile[j].t[:]),
                              ctile[j].rdeps() + tgt.begin())
                    tgt.wrote(cp); ctile[j].read(cp)
        else:
            ldw = wsel.wrote(kb.dma("gpsimd", wsel.t[:, :, 0:64],
                                    W["win"][:, 8448:8512].rearrange("(a p) n -> p a n", p=128),
                                    deps=wsel.begin()))
            dw = dtr.begin()
            for tq in range(4):
                pbk = st.pb[tq % 2]
                sl = slice(tq * 512, (tq + 1) * 512)
                war = pbk.begin()
                mml = None
                for ct in range(NCT):
                    mml = pbk.wrote(kb.mm(pbk.t[0:64, :], wsel.t[:, ct, 0:64], h[ct].t[:, sl],
                                          start=(ct == 0), stop=(ct == NCT - 1),
                                          deps=[ldw] + h[ct].rdeps() + (war if ct == 0 else [])))
                    h[ct].read(mml); wsel.read(mml)
                ev = kb.act(dtr.t[0:64, sl], pbk.t[0:64, :], AF.Identity, bias=dtb_sb.t[:],
                            deps=[mml, ldb] + dw)
                pbk.read(ev); dtr.wrote(ev); dtb_sb.read(ev)
    kb.marks = getattr(kb, "marks", {}); kb.marks["ssd_P"] = len(kb.ops)
    s1 = kb.act(e_T.t[0:64, :], dtr.t[0:64, :], AF.Abs, deps=dtr.rdeps() + e_T.begin())
    e_T.wrote(s1); dtr.read(s1)
    s2 = kb.act(e_T.t[0:64, :], e_T.t[0:64, :], AF.Exp, scale=-1.0, deps=[s1])
    e_T.wrote(s2)
    s3 = kb.act(e_T.t[0:64, :], e_T.t[0:64, :], AF.Ln, bias=one64[:], deps=[s2, one64_ld])
    e_T.wrote(s3)
    s4 = kb.v(lambda e: e.tensor_relu(g_T.t[0:64, :], dtr.t[0:64, :]),
              dtr.rdeps() + g_T.begin())
    g_T.wrote(s4); dtr.read(s4)
    s5 = kb.v(lambda e: e.tensor_add(dtr.t[0:64, :], e_T.t[0:64, :], g_T.t[0:64, :]),
              [s3, s4] + dtr.begin())
    dtr.wrote(s5); e_T.read(s5); g_T.read(s5)
    la = alog_sb.wrote(kb.dma("sync", alog_sb.t[:], W["alog"], deps=alog_sb.begin()))
    na = kb.act(alog_sb.t[:], alog_sb.t[:], AF.Exp, scale=1.0, deps=[la])
    alog_sb.wrote(na)
    na2 = kb.v(lambda e: e.tensor_scalar_mul(alog_sb.t[:], alog_sb.t[:], -1.0), [na])
    alog_sb.wrote(na2)
    a1 = kb.v(lambda e: e.tensor_scalar_mul(acs.t[0:64, :], dtr.t[0:64, :], alog_sb.t[:]),
              [(na2, True), s5] + acs.begin())
    acs.wrote(a1); alog_sb.read(a1); dtr.read(a1)
    onescan, onescan_ld = st.const("c_ones64c", np.ones((64, 1), np.float32), BF16)
    a2 = kb.v(lambda e: e.tensor_tensor_scan(
        acs.t[0:64, :], onescan[:].broadcast_to([64, T]), acs.t[0:64, :], 0.0,
        OP.mult, OP.add), [a1, onescan_ld])
    acs.wrote(a2)
    reb = a2
    for c in range(SSD_NCH - 1, 0, -1):
        c0 = c * SSD_CH
        reb = kb.v(lambda e, c0=c0: e.tensor_scalar(
            acs.t[0:64, c0:c0 + SSD_CH], acs.t[0:64, c0:c0 + SSD_CH],
            acs.t[0:64, c0 - 1:c0], None, OP.subtract), [reb])
        acs.wrote(reb)
    ew = e_T.begin(); gw = g_T.begin()
    for c in range(SSD_NCH):
        c0 = c * SSD_CH
        o1 = kb.act(e_T.t[0:64, c0:c0 + SSD_CH], acs.t[0:64, c0:c0 + SSD_CH], AF.Exp,
                    deps=acs.rdeps() + ew)
        e_T.wrote(o1)
        o2 = kb.act(g_T.t[0:64, c0:c0 + SSD_CH], acs.t[0:64, c0:c0 + SSD_CH], AF.Exp,
                    scale=-1.0, bias=acs.t[0:64, c0 + SSD_CH - 1:c0 + SSD_CH],
                    deps=acs.rdeps() + gw)
        g_T.wrote(o2)
    al1 = kb.v(lambda e: e.tensor_copy(alpha.t[:], e_T.t[0:64, SSD_CH - 1:T:SSD_CH]),
               e_T.rdeps() + alpha.begin())
    alpha.wrote(al1)
    ctw = acstok.begin(); gtw = gtok.begin(); dtw = dttok.begin()
    for c in range(SSD_NCH):
        for stl in range(2):
            c0 = c * SSD_CH + stl * 128
            off = (c * 2 + stl) * 64
            for (srcb, dstb, wdep) in ((acs, acstok, ctw), (g_T, gtok, gtw),
                                       (dtr, dttok, dtw)):
                pT = st.pb[2]
                t1 = pT.wrote(kb.trans(pT.t[0:128, 0:64], srcb.t[0:64, c0:c0 + 128],
                                       idf32[:], deps=srcb.rdeps() + [idf32_ld] + pT.begin()))
                srcb.read(t1)
                cp = pT.read(kb.v(lambda e, off=off, pT=pT, dstb=dstb: e.tensor_copy(
                    dstb.t[:, off:off + 64], pT.t[0:128, 0:64]), [t1] + wdep))
                dstb.wrote(cp)
    a32.release(m32a, [dtr])
    kb.marks["ssd_A"] = len(kb.ops)
    ar.release(hmark, h)
    cmark = ar.mark()
    S_sb = ar.alloc([128, 64 * 64])
    xch = ar.alloc([128, 32, 256])
    xtok = ar.alloc([128, 64, 128])
    gt_sb = ar.alloc([128, 2, 256])
    btok = ar.alloc([128, 2, 128])
    w2t = ar.alloc([128, 2, 256])
    e2 = ar.alloc([128, 384])
    ebc = ar.alloc([128, 256])
    csc = ar.alloc([128, 256])
    xdt = ar.alloc([128, 2, 64])
    bdec = ar.alloc([128, 2, 128])
    ytl = ar.alloc([64, 256])
    c32mark = a32.mark()
    S32 = a32.alloc([128, 64 * 64])
    e2f = a32.alloc([128, 384])
    z0s = kb.v(lambda e: e.memset(S32.t[:], 0.0), S32.begin())
    z0b = kb.v(lambda e: e.memset(S_sb.t[:], 0.0), S_sb.begin())
    S32w = {hh: z0s for hh in range(64)}
    Sw = {hh: z0b for hh in range(64)}
    Sr = {hh: z0b for hh in range(64)}
    y_stores = []
    for c in range(SSD_NCH):
        c0 = c * SSD_CH
        gtw2 = gt_sb.begin()
        for stl in range(2):
            pbk = st.pb[stl]
            mmg = pbk.wrote(kb.mm(pbk.t[:, 0:SSD_CH],
                                  BT.t[:, c0 + stl * 128:c0 + stl * 128 + 128],
                                  CTb.t[:, c0:c0 + SSD_CH], start=True, stop=True,
                                  deps=BT.rdeps() + CTb.rdeps() + pbk.begin()))
            BT.read(mmg); CTb.read(mmg)
            ev = kb.act(gt_sb.t[:, stl, :], pbk.t[:, 0:SSD_CH], AF.Identity,
                        deps=[mmg] + gtw2)
            pbk.read(ev); gt_sb.wrote(ev)
        bw = btok.begin()
        for stl in range(2):
            pb = st.pbf[stl % 2]
            tr = pb.wrote(kb.trans(pb.t[:, 0:128],
                                   BT.t[:, c0 + stl * 128:c0 + stl * 128 + 128],
                                   identb[:], deps=BT.rdeps() + [ident_ld] + pb.begin()))
            BT.read(tr)
            cp = pb.read(kb.v(lambda e, stl=stl, pb=pb: e.tensor_copy(
                btok.t[:, stl, :], pb.t[:, 0:128]), [tr] + bw))
            btok.wrote(cp)
        ldx = xch.wrote(kb.dma("sync", xch.t[:],
                               W["x_scr"][:, c0:c0 + SSD_CH].rearrange("(a p) n -> p a n", p=128),
                               deps=xch.begin() + [x_stores[a] for a in range(32)]))
        xw = xtok.begin()
        for pr in range(32):
            for stl in range(2):
                pb = st.pbf[(pr * 2 + stl) % 2]
                tr = pb.wrote(kb.trans(pb.t[:, 0:128],
                                       xch.t[:, pr, stl * 128:stl * 128 + 128], identb[:],
                                       deps=[ldx, ident_ld] + pb.begin()))
                xch.read(tr)
                if (pr + stl) % 2:
                    cp = kb.act(xtok.t[:, pr * 2 + stl, :], pb.t[:, 0:128], AF.Identity,
                                deps=[tr] + xw)
                else:
                    cp = kb.v(lambda e, pr=pr, stl=stl, pb=pb: e.tensor_copy(
                        xtok.t[:, pr * 2 + stl, :], pb.t[:, 0:128]), [tr] + xw)
                pb.read(cp); xtok.wrote(cp)
        for hh in range(64):
            pr, half = hh // 2, (hh % 2) * 64
            sel1 = kb.op("gpsimd", lambda e, hh=hh: e.affine_select(
                selh.t[:], onesf[:], [[0, 128]], OP.is_equal, 0.0,
                base=-hh, channel_multiplier=1), selh.begin() + [onesf_ld])
            selh.wrote(sel1)
            pP = st.pb[2 + hh % 2]
            mmP = pP.wrote(kb.mm(pP.t[:, 0:SSD_CH], selh.t[:],
                                 acs.t[0:64, c0:c0 + SSD_CH], start=True, stop=True,
                                 deps=acs.rdeps() + [sel1] + pP.begin()))
            acs.read(mmP); selh.read(mmP)
            pS = st.pb[4]
            eb = kb.act(ebc.t[:], pP.t[:, 0:SSD_CH], AF.Exp,
                        deps=[mmP] + ebc.begin())
            pP.read(eb); ebc.wrote(eb)
            cs = kb.v(lambda e, c0=c0: e.tensor_mul(csc.t[:], CTb.t[:, c0:c0 + SSD_CH],
                                                    ebc.t[:]),
                      CTb.rdeps() + [eb] + csc.begin())
            CTb.read(cs); ebc.read(cs); csc.wrote(cs)
            ew2 = e2f.begin(); ee2 = e2.begin(); ww2 = w2t.begin()
            o_w2t = []
            for stl in range(2):
                lo = 0 if stl == 0 else 128
                width = 256 if stl == 0 else 128
                eo = 0 if stl == 0 else 256
                msl = emask[:, 128:384] if stl == 0 else emask[:, 128:256]
                scol = acstok.t[:, (c * 2 + stl) * 64 + hh:(c * 2 + stl) * 64 + hh + 1]
                e0 = kb.v(lambda e, eo=eo, lo=lo, width=width, msl=msl, scol=scol,
                          pP=pP: e.scalar_tensor_tensor(e2f.t[:, eo:eo + width],
                                                        pP.t[:, lo:lo + width], scol, msl,
                                                        OP.subtract, OP.add),
                          [mmP, emask_ld] + acstok.rdeps() + ew2)
                e2f.wrote(e0); pP.read(e0); acstok.read(e0)
                x1 = kb.act(e2.t[:, eo:eo + width], e2f.t[:, eo:eo + width], AF.Exp,
                            deps=[e0] + ee2)
                e2f.read(x1); e2.wrote(x1)
                wt = kb.op("vector" if stl == 0 else "gpsimd",
                           lambda e, stl=stl, lo=lo, width=width, eo=eo:
                           e.tensor_mul(w2t.t[:, stl, lo:lo + width],
                                        gt_sb.t[:, stl, lo:lo + width],
                                        e2.t[:, eo:eo + width]),
                           gt_sb.rdeps() + [x1] + ww2)
                gt_sb.read(wt); e2.read(wt); w2t.wrote(wt)
                o_w2t.append(wt)
            xd = xdt.begin()
            o_xdt = []
            for stl in range(2):
                dcol = dttok.t[:, (c * 2 + stl) * 64 + hh:(c * 2 + stl) * 64 + hh + 1]
                o = kb.op("gpsimd", lambda e, stl=stl, pr=pr, half=half, dcol=dcol:
                          e.tensor_scalar_mul(xdt.t[:, stl, :],
                                              xtok.t[:, pr * 2 + stl, half:half + 64], dcol),
                          xtok.rdeps() + dttok.rdeps() + xd)
                xdt.wrote(o); xtok.read(o); dttok.read(o)
                o_xdt.append(o)
            pY = st.pb[5]
            yw = pY.begin()
            my1 = pY.wrote(kb.mm(pY.t[0:64, 0:SSD_CH], xdt.t[:, 0, :], w2t.t[:, 0, :],
                                 start=True, stop=False,
                                 deps=[o_xdt[0], o_w2t[0]] + yw))
            xdt.read(my1); w2t.read(my1)
            my2 = pY.wrote(kb.mm(pY.t[0:64, 128:SSD_CH], xdt.t[:, 1, :],
                                 w2t.t[:, 1, 128:SSD_CH], start=False, stop=False,
                                 deps=[o_xdt[1], o_w2t[1]]))
            xdt.read(my2); w2t.read(my2)
            my3 = pY.wrote(kb.mm(pY.t[0:64, 0:SSD_CH], S_sb.t[:, hh * 64:(hh + 1) * 64],
                                 csc.t[:], start=False, stop=True,
                                 deps=[cs, Sw[hh]]))
            csc.read(my3)
            Sr[hh] = my3
            ye = kb.act(ytl.t[0:64, :], pY.t[0:64, 0:SSD_CH], AF.Identity,
                        deps=[my3] + ytl.begin())
            pY.read(ye); ytl.wrote(ye)
            ys = kb.dma("sync", W["y_scr"][hh * 64:(hh + 1) * 64, c0:c0 + SSD_CH],
                        ytl.t[0:64, :], deps=[ye])
            ytl.read(ys)
            y_stores.append(ys)
            mma = pS.wrote(kb.mm(pS.t[:, 0:1], selh.t[:],
                                 alpha.t[:, c:c + 1], start=True, stop=True,
                                 deps=alpha.rdeps() + [sel1] + pS.begin()))
            alpha.read(mma); selh.read(mma)
            ae = kb.act(aco.t[:], pS.t[:, 0:1], AF.Identity, deps=[mma] + aco.begin())
            pS.read(ae); aco.wrote(ae)
            bw2 = bdec.begin()
            od = []
            for stl in range(2):
                gcol = gtok.t[:, (c * 2 + stl) * 64 + hh:(c * 2 + stl) * 64 + hh + 1]
                o = kb.op("gpsimd", lambda e, stl=stl, gcol=gcol:
                          e.tensor_scalar_mul(bdec.t[:, stl, :], btok.t[:, stl, :], gcol),
                          btok.rdeps() + gtok.rdeps() + bw2)
                bdec.wrote(o); btok.read(o); gtok.read(o)
                od.append(o)
            pD = st.pb[hh % 2]
            md1 = pD.wrote(kb.mm(pD.t[:, 0:64], bdec.t[:, 0, :], xdt.t[:, 0, :],
                                 start=True, stop=False,
                                 deps=[od[0], o_xdt[0]] + pD.begin()))
            bdec.read(md1); xdt.read(md1)
            md2 = pD.wrote(kb.mm(pD.t[:, 0:64], bdec.t[:, 1, :], xdt.t[:, 1, :],
                                 start=False, stop=True, deps=[od[1], o_xdt[1]]))
            bdec.read(md2); xdt.read(md2)
            su = kb.v(lambda e, hh=hh, pD=pD: e.scalar_tensor_tensor(
                S32.t[:, hh * 64:(hh + 1) * 64], S32.t[:, hh * 64:(hh + 1) * 64],
                aco.t[:], pD.t[:, 0:64], OP.mult, OP.add),
                [S32w[hh], md2, (ae, True)])
            pD.read(su); aco.read(su)
            S32w[hh] = su
            sc2 = kb.v(lambda e, hh=hh: e.tensor_copy(
                S_sb.t[:, hh * 64:(hh + 1) * 64], S32.t[:, hh * 64:(hh + 1) * 64]),
                [su, Sr[hh], Sw[hh]])
            Sw[hh] = sc2
    kb.marks["ssd_C"] = len(kb.ops)
    ar.release(cmark, [S_sb, xch, xtok, gt_sb, btok, w2t, e2, ebc, csc, xdt, bdec, ytl])
    a32.release(c32mark, [S32, e2f])
    gmark = ar.mark()
    yt = [ar.alloc([128, T]) for _ in range(2)]
    zt = [ar.alloc([128, T]) for _ in range(2)]
    xt = [ar.alloc([128, T]) for _ in range(2)]
    sqt = [ar.alloc([128, T]) for _ in range(2)]
    msrow = a32.alloc([1, T])
    dful_sb = SB(sbuf(ctx, nc, f"{name}_dful", [128, 32], F32))
    ldd = dful_sb.wrote(kb.dma("sync", dful_sb.t[:], W["dful"][:, :],
                               deps=dful_sb.begin()))
    nsc_sb = SB(sbuf(ctx, nc, f"{name}_nsc", [128, 32], F32))
    ldn = nsc_sb.wrote(kb.dma("sync", nsc_sb.t[:], W["nsc"][:, :],
                              deps=nsc_sb.begin()))
    stat_war = [st.pb[2 + i].begin() for i in range(4)]
    stat_last = [None] * 4
    yg_stores = []
    for ct in range(32):
        j = ct % 2
        l1 = yt[j].wrote(kb.dma("sync", yt[j].t[:], W["y_scr"][ct * 128:(ct + 1) * 128, :],
                                deps=yt[j].begin() +
                                [y_stores[c * 64 + 2 * ct + k] for c in range(SSD_NCH)
                                 for k in range(2)]))
        l2 = zt[j].wrote(kb.dma("sync", zt[j].t[:], W["z_scr"][ct * 128:(ct + 1) * 128, :],
                                deps=zt[j].begin() + [z_stores[ct]]))
        l3 = xt[j].wrote(kb.dma("sync", xt[j].t[:], W["x_scr"][ct * 128:(ct + 1) * 128, :],
                                deps=xt[j].begin() + [x_stores[ct]]))
        sl1 = kb.act(zt[j].t[:], zt[j].t[:], AF.Silu, deps=[l2])
        zt[j].wrote(sl1)
        g1 = kb.v(lambda e, j=j, ct=ct: e.scalar_tensor_tensor(
            yt[j].t[:], xt[j].t[:], dful_sb.t[:, ct:ct + 1], yt[j].t[:],
            OP.mult, OP.add), [l1, l3, (ldd, True)])
        yt[j].wrote(g1); xt[j].read(g1); dful_sb.read(g1)
        g2 = kb.v(lambda e, j=j: e.tensor_mul(yt[j].t[:], yt[j].t[:], zt[j].t[:]),
                  [g1, sl1])
        yt[j].wrote(g2); zt[j].read(g2)
        sq = kb.act(sqt[j].t[:], yt[j].t[:], AF.Square, deps=[g2] + sqt[j].begin())
        sqt[j].wrote(sq); yt[j].read(sq)
        for tq in range(4):
            pbk = st.pb[2 + tq]
            mm = pbk.wrote(kb.mm(pbk.t[0:1, :], onesc[:], sqt[j].t[:, tq * 512:(tq + 1) * 512],
                                 start=(ct == 0), stop=(ct == 31),
                                 deps=[sq, onesc_ld] + (stat_war[tq] if ct == 0 else [])))
            sqt[j].read(mm)
            stat_last[tq] = mm
        g3 = kb.v(lambda e, j=j, ct=ct: e.tensor_scalar_mul(
            yt[j].t[:], yt[j].t[:], nsc_sb.t[:, ct:ct + 1]), [(sq, True), g2, (ldn, True)])
        yt[j].wrote(g3); nsc_sb.read(g3)
        s1d = kb.dma("sync", W["yg_scr"][ct * 128:(ct + 1) * 128, :], yt[j].t[:], deps=[g3])
        yt[j].read(s1d)
        yg_stores.append(s1d)
    mw = msrow.begin()
    ms_ops = []
    for tq in range(4):
        pbk = st.pb[2 + tq]
        ev = kb.act(msrow.t[0:1, tq * 512:(tq + 1) * 512], pbk.t[0:1, :], AF.Identity,
                    scale=1.0 / 4096.0, deps=[stat_last[tq]] + mw)
        pbk.read(ev); msrow.wrote(ev); ms_ops.append(ev)
    r1 = kb.v(lambda e: e.tensor_scalar_add(msrow.t[0:1, :], msrow.t[0:1, :], 1e-5), ms_ops)
    msrow.wrote(r1)
    r2 = kb.act(msrow.t[0:1, :], msrow.t[0:1, :], AF.Sqrt, deps=[r1])
    msrow.wrote(r2)
    r3 = kb.v(lambda e: e.reciprocal(msrow.t[0:1, :], msrow.t[0:1, :]), [r2])
    msrow.wrote(r3)
    ar.release(gmark, yt + zt + xt + sqt)
    rbf = ar.alloc([1, T])
    rb1 = kb.v(lambda e: e.tensor_copy(rbf.t[0:1, :], msrow.t[0:1, :]),
               [r3] + rbf.begin())
    rbf.wrote(rb1); msrow.read(rb1)
    ygch = ar.alloc([128, 32, 512])
    wot = [ar.alloc([128, 32, 128]) for _ in range(2)]
    rbc_sb = ar.alloc([128, 512])
    xres = [a32.alloc([128, 512]) for _ in range(2)]
    o32 = [a32.alloc([128, 512]) for _ in range(2)]
    for tq in range(4):
        sl = slice(tq * 512, (tq + 1) * 512)
        pR = st.pb[2]
        rmm = pR.wrote(kb.mm(pR.t[:], onesr[:], rbf.t[0:1, sl], start=True, stop=True,
                             deps=rbf.rdeps() + [onesr_ld] + pR.begin()))
        rbf.read(rmm)
        rev = kb.act(rbc_sb.t[:], pR.t[:], AF.Identity, deps=[rmm] + rbc_sb.begin())
        pR.read(rev); rbc_sb.wrote(rev)
        ldyg = ygch.wrote(kb.dma("sync", ygch.t[:],
                                 W["yg_scr"][:, sl].rearrange("(a p) n -> p a n", p=128),
                                 deps=ygch.begin() + yg_stores))
        for co in range(16):
            j = co % 2
            ldw = wot[j].wrote(kb.dma("gpsimd", wot[j].t[:],
                                      W["wout"][:, co * 128:(co + 1) * 128].rearrange(
                                          "(a p) n -> p a n", p=128),
                                      deps=wot[j].begin()))
            ldxr = xres[j].wrote(kb.dma("sync", xres[j].t[:], x_hbm[co * 128:(co + 1) * 128, sl],
                                        deps=xres[j].begin()))
            pO = st.pb[3 + (co % 2)]
            ow = pO.begin()
            mml = None
            for kt in range(32):
                mml = pO.wrote(kb.mm(pO.t[:], wot[j].t[:, kt, :], ygch.t[:, kt, :],
                                     start=(kt == 0), stop=(kt == 31),
                                     deps=[ldw, ldyg] + (ow if kt == 0 else [])))
                wot[j].read(mml); ygch.read(mml)
            t1 = kb.v(lambda e, j=j, pO=pO: e.tensor_mul(o32[j].t[:], pO.t[:], rbc_sb.t[:]),
                      [mml] + rbc_sb.rdeps() + o32[j].begin())
            pO.read(t1); rbc_sb.read(t1); o32[j].wrote(t1)
            t2 = kb.v(lambda e, j=j: e.tensor_add(o32[j].t[:], o32[j].t[:], xres[j].t[:]),
                      [t1, ldxr])
            o32[j].wrote(t2); xres[j].read(t2)
            sd = kb.dma("sync", xout_hbm[co * 128:(co + 1) * 128, sl], o32[j].t[:], deps=[t2])
            o32[j].read(sd)

def stage_l1ssd(inp):
    w = {}
    w["win"] = np.ascontiguousarray(inp["L1_attn_a_W_in"])
    w["convwT"] = np.ascontiguousarray(inp["L1_attn_a_conv_w"].T)
    w["convb"] = np.ascontiguousarray(inp["L1_attn_a_conv_b"][:, None])
    w["dtb"] = np.ascontiguousarray(inp["L1_attn_a_dt_bias"][:, None])
    w["alog"] = np.ascontiguousarray(inp["L1_attn_a_A_log"][:, None])
    w["dful"] = np.ascontiguousarray(
        np.broadcast_to(inp["L1_attn_a_D"][:, None], (64, 64)).reshape(32, 128).T)
    w["nsc"] = np.ascontiguousarray(inp["L1_attn_a_norm_scale"].reshape(32, 128).T)
    w["wout"] = np.ascontiguousarray(inp["L1_attn_a_W_out"])
    w["alpha_dummy"] = None
    del w["alpha_dummy"]
    return w

def build_l1ssd_program(trunc=None):
    nc = bass.Bass("TRN2", target_bir_lowering=False, debug=False)
    xin = nc.dram_tensor("xin", [H, T], F32, kind="ExternalInput").ap()
    W = {}
    W["win"] = nc.dram_tensor("win", [H, 8512], F32, kind="ExternalInput").ap()
    W["convwT"] = nc.dram_tensor("convwT", [4352, 4], F32, kind="ExternalInput").ap()
    W["convb"] = nc.dram_tensor("convb", [4352, 1], F32, kind="ExternalInput").ap()
    W["dtb"] = nc.dram_tensor("dtb", [64, 1], F32, kind="ExternalInput").ap()
    W["alog"] = nc.dram_tensor("alog", [64, 1], F32, kind="ExternalInput").ap()
    W["dful"] = nc.dram_tensor("dful", [128, 32], F32, kind="ExternalInput").ap()
    W["nsc"] = nc.dram_tensor("nsc", [128, 32], F32, kind="ExternalInput").ap()
    W["wout"] = nc.dram_tensor("wout", [4096, H], F32, kind="ExternalInput").ap()
    W["x_scr"] = nc.dram_tensor("ssd_xscr", [4096, T], BF16, kind="Internal").ap()
    W["z_scr"] = nc.dram_tensor("ssd_zscr", [4096, T], BF16, kind="Internal").ap()
    W["y_scr"] = nc.dram_tensor("ssd_yscr", [4096, T], BF16, kind="Internal").ap()
    W["yg_scr"] = nc.dram_tensor("ssd_ygscr", [4096, T], BF16, kind="Internal").ap()
    xout = nc.dram_tensor("xout", [H, T], F32, kind="ExternalOutput").ap()
    ctx = ExitStack()
    with ctx:
        kb = KB(nc)
        st = St(nc, kb, ctx, arena_words=53248, arena32_words=15872)
        emit_l1ssd(st, xin, xout, W)
        build_l1ssd_program.marks = dict(getattr(kb, "marks", {}))
        build_l1ssd_program.nops = len(kb.ops)
        if trunc is not None:
            kb.ops = kb.ops[:trunc]
        with nc.Block() as block:
            kb.emit(block, ctx)
    return nc

DN_C = 128
DN_NCH = T // DN_C

def emit_l0dn(st, x_hbm, xout_hbm, W, name="dn"):
    nc, kb, ctx = st.nc, st.kb, st.ctx
    ar = st.arena
    a32 = st.arena32
    onesr, onesr_ld = st.ones_row()
    onesc, onesc_ld = st.ones_col()
    identb, ident_ld = st.ident()
    idf32, idf32_ld = st.const("c_idf32b", np.eye(32, dtype=np.float32), F32)
    tl = np.tril(np.ones((DN_C, DN_C), np.float32), -1)
    maskL, maskL_ld = st.const("c_dn_maskL", tl, BF16)
    maskU, maskU_ld = st.const("c_dn_maskU", np.triu(np.ones((DN_C, DN_C), np.float32), 0), BF16)
    lnab = SB(sbuf(ctx, nc, f"{name}_lnab", [1, 2 * T], F32))
    lnab_bf = SB(sbuf(ctx, nc, f"{name}_lnabbf", [1, 2 * T], BF16))
    wrow = {}
    wrow_ld = {}
    for nm in ("anw", "onw", "mnw"):
        wrow[nm] = SB(sbuf(ctx, nc, f"{name}_{nm}", [1, T], BF16))
        wrow_ld[nm] = wrow[nm].wrote(kb.dma("gpsimd", wrow[nm].t[:], W[nm],
                                            deps=wrow[nm].begin()))
    btok = SB(sbuf(ctx, nc, f"{name}_btok", [128, DN_NCH * 32], F32))

    def post_rms(stx, rows, deps):
        r_inv = rows.bufs["inv"]
        r_m = rows.row("sum")
        s1 = kb.v(lambda e: e.tensor_copy(lnab.t[0:1, 0:T], r_inv.t[:]),
                  deps + lnab.begin())
        lnab.wrote(s1)
        s2 = kb.v(lambda e: e.tensor_mul(lnab.t[0:1, T:2 * T], r_inv.t[:], r_m.t[:]),
                  deps + [s1])
        lnab.wrote(s2)
        s3 = kb.v(lambda e: e.tensor_scalar_mul(lnab.t[0:1, T:2 * T],
                                                lnab.t[0:1, T:2 * T], -1.0), [s2])
        lnab.wrote(s3)
        s4 = kb.v(lambda e: e.tensor_copy(lnab_bf.t[:], lnab.t[:]),
                  [s3] + lnab_bf.begin())
        lnab_bf.wrote(s4)
        o1 = kb.v(lambda e: e.tensor_mul(r_inv.t[:], r_inv.t[:], r_inv.t[:]),
                  deps + [(s4, True)])
        o2 = kb.v(lambda e: e.reciprocal(r_inv.t[:], r_inv.t[:]), [o1])
        o3 = kb.v(lambda e: e.tensor_scalar_add(r_inv.t[:], r_inv.t[:], -1e-5), [o2])
        o4 = kb.v(lambda e: e.tensor_scalar(r_inv.t[:], r_inv.t[:], 1.0 + 1e-6, 1e-11,
                                            OP.mult, OP.add), [o3])
        o5 = kb.act(r_inv.t[:], r_inv.t[:], AF.Sqrt, deps=[(o4, True)])
        o6 = kb.v(lambda e: e.reciprocal(r_inv.t[:], r_inv.t[:]), [o5])
        return o6

    hmark = ar.mark()
    h = [ar.alloc([128, T]) for _ in range(NCT)]
    emit_ln(st, x_hbm, w_row=(lambda i: wrow["anw"].t[0:1, i * 128:(i + 1) * 128],
                              wrow_ld["anw"]),
            post=post_rms, name=name + "_ln", h_tiles=h)
    st.dn_lnab_bf = lnab_bf
    st.dn_wrow = wrow
    st.dn_wrow_ld = wrow_ld
    st.dn_lnab_op = lnab_bf.writers[-1]
    bt_rows = a32.alloc([32, T])
    nrm = a32.alloc([1, 2 * T])
    wcol = [ar.alloc([128, NCT, 128]) for j in range(2)]
    ptile = [ar.alloc([128, T + 4]) for j in range(2)]
    ctile = [ar.alloc([128, T]) for j in range(2)]
    sqt2 = [ar.alloc([128, T]) for j in range(2)]
    nrm_bf = ar.alloc([1, 2 * T])
    cvw = [SB(sbuf(ctx, nc, f"{name}_cvw{j}", [128, 4], F32)) for j in range(2)]
    scr_stores = {}
    for wname, cname, scrname, do_norm in (("wqT", "conv_q", "q_scr", True),
                                           ("wkT", "conv_k", "k_scr", True),
                                           ("wvT", "conv_v", "v_scr", False)):
        for ot in range(NCT):
            j = ot % 2
            c0 = ot * 128
            wsel = wcol[j]
            ldw = wsel.wrote(kb.dma("gpsimd", wsel.t[:],
                                    W[wname][:, c0:c0 + 128].rearrange("(a p) n -> p a n", p=128),
                                    deps=wsel.begin()))
            dst = ptile[j]
            dw = dst.begin()
            z0 = kb.v(lambda e, j=j: e.memset(ptile[j].t[:, 0:4], 0.0), dw)
            dst.wrote(z0)
            for tq in range(4):
                pbk = st.pb[tq % 2]
                sl = slice(tq * 512, (tq + 1) * 512)
                war = pbk.begin()
                mml = None
                for ct in range(NCT):
                    mml = pbk.wrote(kb.mm(pbk.t[:], wsel.t[:, ct, :], h[ct].t[:, sl],
                                          start=(ct == 0), stop=(ct == NCT - 1),
                                          deps=[ldw] + h[ct].rdeps() + (war if ct == 0 else [])))
                    h[ct].read(mml); wsel.read(mml)
                ev = kb.act(dst.t[:, 4 + tq * 512:4 + (tq + 1) * 512], pbk.t[:],
                            AF.Identity, deps=[mml] + dw)
                pbk.read(ev); dst.wrote(ev)
            lw = cvw[j].wrote(kb.dma("sync", cvw[j].t[:], W[cname][c0:c0 + 128, :],
                                     deps=cvw[j].begin()))
            cw = ctile[j].begin()
            o1 = kb.v(lambda e, j=j: e.tensor_scalar_mul(
                ctile[j].t[:], ptile[j].t[:, 1:1 + T], cvw[j].t[:, 0:1]),
                dst.rdeps() + [(lw, True)] + cw)
            ctile[j].wrote(o1); dst.read(o1); cvw[j].read(o1)
            for i in range(1, 4):
                o1 = kb.v(lambda e, j=j, i=i: e.scalar_tensor_tensor(
                    ctile[j].t[:], ptile[j].t[:, 1 + i:1 + i + T], cvw[j].t[:, i:i + 1],
                    ctile[j].t[:], OP.mult, OP.add), [o1])
                ctile[j].wrote(o1); dst.read(o1); cvw[j].read(o1)
            sil = kb.act(ctile[j].t[:], ctile[j].t[:], AF.Silu, deps=[o1])
            ctile[j].wrote(sil)
            sil = kb.act(ctile[j].t[:], ctile[j].t[:], AF.Silu, deps=[sil])
            ctile[j].wrote(sil)
            if do_norm:
                sq = kb.act(sqt2[j].t[:], ctile[j].t[:], AF.Square,
                            deps=[sil] + sqt2[j].begin())
                sqt2[j].wrote(sq)
                nw = nrm.begin() if True else []
                ev_ops = []
                for tq in range(4):
                    sl = slice(tq * 512, (tq + 1) * 512)
                    for hb in range(2):
                        pbk = st.pb[4 + hb]
                        b0 = hb * 64
                        mmn = pbk.wrote(kb.mm(pbk.t[0:1, :], onesc[b0:b0 + 64, 0:1],
                                              sqt2[j].t[b0:b0 + 64, sl], start=True, stop=True,
                                              deps=[sq, onesc_ld] + pbk.begin()))
                        sqt2[j].read(mmn)
                        evn = kb.v(lambda e, hb=hb, tq=tq, pbk=pbk: e.tensor_copy(
                            nrm.t[0:1, hb * T + tq * 512:hb * T + (tq + 1) * 512],
                            pbk.t[0:1, :]), [mmn] + (nw if tq == 0 and hb == 0 else []))
                        pbk.read(evn); nrm.wrote(evn)
                        ev_ops.append(evn)
                r1 = kb.act(nrm.t[:], nrm.t[:], AF.Sqrt, deps=ev_ops)
                nrm.wrote(r1)
                r2 = kb.v(lambda e: e.tensor_scalar(nrm.t[:], nrm.t[:], 1e-12, None,
                                                    OP.max), [r1])
                nrm.wrote(r2)
                r3 = kb.v(lambda e: e.reciprocal(nrm.t[:], nrm.t[:]), [r2])
                nrm.wrote(r3)
                r4 = kb.v(lambda e: e.tensor_copy(nrm_bf.t[:], nrm.t[:]),
                          [r3] + nrm_bf.begin())
                nrm_bf.wrote(r4)
                for tq in range(4):
                    sl = slice(tq * 512, (tq + 1) * 512)
                    for hb in range(2):
                        pbk = st.pb[2 + (2 * tq + hb) % 2]
                        b0 = hb * 64
                        mmb = pbk.wrote(kb.mm(pbk.t[:], onesr[:],
                                              nrm_bf.t[0:1, hb * T + tq * 512:hb * T + (tq + 1) * 512],
                                              start=True, stop=True,
                                              deps=[r4, onesr_ld] + pbk.begin()))
                        nrm_bf.read(mmb)
                        tt = kb.v(lambda e, j=j, b0=b0, sl=sl, pbk=pbk: e.tensor_mul(
                            ctile[j].t[b0:b0 + 64, sl], ctile[j].t[b0:b0 + 64, sl],
                            pbk.t[b0:b0 + 64, :]), [mmb, sil] + ctile[j].war())
                        ctile[j].wrote(tt); pbk.read(tt)
            so = kb.dma("sync", W[scrname][c0:c0 + 128, :], ctile[j].t[:],
                        deps=ctile[j].rdeps())
            ctile[j].read(so)
            scr_stores[(scrname, ot)] = so
    wb_sb = ar.alloc([128, NCT, 32])
    ldwb = wb_sb.wrote(kb.dma("gpsimd", wb_sb.t[:],
                              W["wbT"].rearrange("(a p) n -> p a n", p=128),
                              deps=wb_sb.begin()))
    btw = bt_rows.begin()
    for tq in range(4):
        pbk = st.pb[tq % 2]
        sl = slice(tq * 512, (tq + 1) * 512)
        war = pbk.begin()
        mml = None
        for ct in range(NCT):
            mml = pbk.wrote(kb.mm(pbk.t[0:32, :], wb_sb.t[:, ct, :], h[ct].t[:, sl],
                                  start=(ct == 0), stop=(ct == NCT - 1),
                                  deps=[ldwb] + h[ct].rdeps() + (war if ct == 0 else [])))
            h[ct].read(mml); wb_sb.read(mml)
        ev = kb.act(bt_rows.t[0:32, sl], pbk.t[0:32, :], AF.Sigmoid, deps=[mml] + btw)
        pbk.read(ev); bt_rows.wrote(ev)
    bkw = btok.begin()
    for c in range(DN_NCH):
        pT = st.pb[2]
        t1 = pT.wrote(kb.trans(pT.t[0:128, 0:32], bt_rows.t[0:32, c * 128:(c + 1) * 128],
                               idf32[:], deps=bt_rows.rdeps() + [idf32_ld] + pT.begin()))
        bt_rows.read(t1)
        cp = pT.read(kb.v(lambda e, c=c, pT=pT: e.tensor_copy(
            btok.t[:, c * 32:(c + 1) * 32], pT.t[0:128, 0:32]), [t1] + bkw))
        btok.wrote(cp)
    ar.release(hmark, h + wcol + ptile + ctile + sqt2 + [nrm_bf, wb_sb])
    cmark = ar.mark()
    qh = [ar.alloc([64, T]) for _ in range(2)]
    kh = [ar.alloc([64, T]) for _ in range(2)]
    vh = [ar.alloc([64, T]) for _ in range(2)]
    vtok = ar.alloc([128, 64])
    ktok = ar.alloc([128, 64])
    bt_sb = ar.alloc([128, DN_C])
    b_sb = ar.alloc([128, DN_C])
    x_sb = [ar.alloc([128, DN_C]) for _ in range(2)]
    xt_sb = [ar.alloc([128, DN_C]) for _ in range(2)]
    p_sb = [ar.alloc([128, DN_C]) for _ in range(2)]
    u_sb = ar.alloc([128, 64])
    wp_sb = ar.alloc([128, DN_C])
    rhs_sb = ar.alloc([128, 64])
    tmp32 = a32.alloc([128, 64])
    sdv_sb = ar.alloc([64, 64])
    otl = ar.alloc([64, DN_C])
    S32 = a32.alloc([64, 64])
    o_stores = []
    for hh in range(32):
        jh = hh % 2
        pr = hh // 2
        b0 = (hh % 2) * 64
        r0 = hh * 64
        ldq = qh[jh].wrote(kb.dma("sync", qh[jh].t[:], W["q_scr"][r0:r0 + 64, :],
                                  deps=qh[jh].begin() + [scr_stores[("q_scr", hh // 2)]]))
        ldk = kh[jh].wrote(kb.dma("sync", kh[jh].t[:], W["k_scr"][r0:r0 + 64, :],
                                  deps=kh[jh].begin() + [scr_stores[("k_scr", hh // 2)]]))
        ldv = vh[jh].wrote(kb.dma("sync", vh[jh].t[:], W["v_scr"][r0:r0 + 64, :],
                                  deps=vh[jh].begin() + [scr_stores[("v_scr", hh // 2)]]))
        zs = kb.v(lambda e: e.memset(S32.t[:], 0.0), S32.begin())
        S32.wrote(zs)
        zb = kb.v(lambda e: e.memset(sdv_sb.t[:], 0.0), sdv_sb.begin())
        sdv_sb.wrote(zb)
        for c in range(DN_NCH):
            c0 = c * DN_C
            csl = slice(c0, c0 + DN_C)
            bcol = btok.t[:, c * 32 + hh:c * 32 + hh + 1]
            for (src, dst, ld) in ((kh[jh], ktok, ldk), (vh[jh], vtok, ldv)):
                pb = st.pbf[0 if dst is ktok else 1]
                tr = pb.wrote(kb.trans(pb.t[:, 0:64], src.t[0:64, csl], identb[0:64, 0:64],
                                       deps=[ld, ident_ld] + pb.begin()))
                src.read(tr)
                cp = kb.v(lambda e, dst=dst, pb=pb: e.tensor_copy(dst.t[:], pb.t[:, 0:64]),
                          [tr] + dst.begin())
                pb.read(cp); dst.wrote(cp)
            pw = st.pb[0]
            mw = pw.wrote(kb.mm(pw.t[:, 0:DN_C], kh[jh].t[0:64, csl], kh[jh].t[0:64, csl],
                                start=True, stop=True, deps=[ldk] + pw.begin()))
            kh[jh].read(mw)
            bw = bt_sb.begin()
            e1 = kb.v(lambda e, pw=pw, bcol=bcol: e.tensor_scalar(
                bt_sb.t[:], pw.t[:, 0:DN_C], bcol, -1.0, OP.mult, OP.mult),
                [mw] + btok.rdeps() + bw)
            pw.read(e1); btok.read(e1); bt_sb.wrote(e1)
            e2 = kb.v(lambda e: e.tensor_mul(bt_sb.t[:], bt_sb.t[:], maskL[:]),
                      [e1, maskL_ld])
            bt_sb.wrote(e2)
            pt = st.pbf[0]
            tb = pt.wrote(kb.trans(pt.t[:, 0:DN_C], bt_sb.t[:], identb[:],
                                   deps=[e2, ident_ld] + pt.begin()))
            bt_sb.read(tb)
            bw2 = b_sb.begin()
            eb = kb.act(b_sb.t[:], pt.t[:, 0:DN_C], AF.Identity, deps=[tb] + bw2)
            pt.read(eb); b_sb.wrote(eb)
            pw0 = p_sb[0].begin()
            p0 = kb.v(lambda e: e.tensor_add(p_sb[0].t[:], b_sb.t[:], identb[:]),
                      [eb, ident_ld] + pw0)
            p_sb[0].wrote(p0); b_sb.read(p0)
            Xc, Xtc = b_sb, bt_sb
            xop, xtop = eb, e2
            pcur = p_sb[0]
            pop = p0
            for lev in range(1, 7):
                jl = lev % 2
                px = st.pb[2 + jl]
                m1 = px.wrote(kb.mm(px.t[:, 0:DN_C], Xtc.t[:], Xc.t[:],
                                    start=True, stop=True, deps=[xop, xtop] + px.begin()))
                Xtc.read(m1); Xc.read(m1)
                nx = x_sb[jl]
                nxw = nx.begin()
                ex1 = kb.act(nx.t[:], px.t[:, 0:DN_C], AF.Identity, deps=[m1] + nxw)
                px.read(ex1); nx.wrote(ex1)
                px2 = st.pb[4]
                m2 = px2.wrote(kb.mm(px2.t[:, 0:DN_C], Xc.t[:], Xtc.t[:],
                                     start=True, stop=True, deps=[xop, xtop] + px2.begin()))
                Xc.read(m2); Xtc.read(m2)
                nxt = xt_sb[jl]
                nxtw = nxt.begin()
                ex2 = kb.v(lambda e, nxt=nxt, px2=px2: e.tensor_copy(
                    nxt.t[:], px2.t[:, 0:DN_C]), [m2] + nxtw)
                px2.read(ex2); nxt.wrote(ex2)
                pp = st.pb[5]
                m3 = pp.wrote(kb.mm(pp.t[:, 0:DN_C], identb[:], pcur.t[:],
                                    start=True, stop=False, deps=[pop, ident_ld] + pp.begin()))
                pcur.read(m3)
                m4 = pp.wrote(kb.mm(pp.t[:, 0:DN_C], nxt.t[:], pcur.t[:],
                                    start=False, stop=True, deps=[ex2, pop]))
                nxt.read(m4); pcur.read(m4)
                pnew = p_sb[1 - (0 if pcur is p_sb[0] else 1)]
                pnw = pnew.begin()
                pop = kb.act(pnew.t[:], pp.t[:, 0:DN_C], AF.Identity, deps=[m4] + pnw)
                pp.read(pop); pnew.wrote(pop)
                pcur = pnew
                Xc, Xtc = nx, nxt
                xop, xtop = ex1, ex2
            pks = st.pb[0]
            mks = pks.wrote(kb.mm(pks.t[:, 0:64], kh[jh].t[0:64, csl], sdv_sb.t[:],
                                  start=True, stop=True,
                                  deps=[ldk] + sdv_sb.rdeps() + pks.begin()))
            kh[jh].read(mks); sdv_sb.read(mks)
            tw = tmp32.begin()
            t1 = kb.v(lambda e, pks=pks: e.tensor_tensor(
                tmp32.t[:], vtok.t[:], pks.t[:, 0:64], OP.subtract),
                vtok.rdeps() + [mks] + tw)
            pks.read(t1); vtok.read(t1); tmp32.wrote(t1)
            rw = rhs_sb.begin()
            t2 = kb.v(lambda e, bcol=bcol: e.tensor_scalar_mul(
                rhs_sb.t[:], tmp32.t[:], bcol), [t1] + btok.rdeps() + rw)
            tmp32.read(t2); btok.read(t2); rhs_sb.wrote(t2)
            pu = st.pb[1]
            mu_ = pu.wrote(kb.mm(pu.t[:, 0:64], pcur.t[:], rhs_sb.t[:],
                                 start=True, stop=True, deps=[pop, t2] + pu.begin()))
            pcur.read(mu_); rhs_sb.read(mu_)
            uw = u_sb.begin()
            eu = kb.act(u_sb.t[:], pu.t[:, 0:64], AF.Identity, deps=[mu_] + uw)
            pu.read(eu); u_sb.wrote(eu)
            pwp = st.pb[2]
            mwp = pwp.wrote(kb.mm(pwp.t[:, 0:DN_C], kh[jh].t[0:64, csl],
                                  qh[jh].t[0:64, csl], start=True, stop=True,
                                  deps=[ldk, ldq] + pwp.begin()))
            kh[jh].read(mwp); qh[jh].read(mwp)
            ww = wp_sb.begin()
            ewp = kb.v(lambda e, pwp=pwp: e.tensor_tensor(
                wp_sb.t[:], pwp.t[:, 0:DN_C], maskU[:], OP.mult),
                [mwp, maskU_ld] + ww)
            pwp.read(ewp); wp_sb.wrote(ewp)
            po = st.pb[3]
            mo1 = po.wrote(kb.mm(po.t[0:64, 0:DN_C], sdv_sb.t[:], qh[jh].t[0:64, csl],
                                 start=True, stop=False,
                                 deps=sdv_sb.rdeps() + [ldq] + po.begin()))
            sdv_sb.read(mo1); qh[jh].read(mo1)
            mo2 = po.wrote(kb.mm(po.t[0:64, 0:DN_C], u_sb.t[:, 0:64], wp_sb.t[:],
                                 start=False, stop=True, deps=[eu, ewp]))
            u_sb.read(mo2); wp_sb.read(mo2)
            ow = otl.begin()
            eo = kb.act(otl.t[0:64, :], po.t[0:64, 0:DN_C], AF.Identity, deps=[mo2] + ow)
            po.read(eo); otl.wrote(eo)
            os_ = kb.dma("sync", W["o_scr"][r0:r0 + 64, c0:c0 + DN_C], otl.t[0:64, :],
                         deps=[eo])
            otl.read(os_)
            o_stores.append(os_)
            ps = st.pb[4]
            ms = ps.wrote(kb.mm(ps.t[0:64, 0:64], ktok.t[:], u_sb.t[:, 0:64],
                                start=True, stop=True,
                                deps=ktok.rdeps() + [eu] + ps.begin()))
            ktok.read(ms); u_sb.read(ms)
            su = kb.v(lambda e, ps=ps: e.tensor_add(S32.t[:], S32.t[:], ps.t[0:64, 0:64]),
                      S32.rdeps() + [ms, (zb, True)])
            ps.read(su); S32.wrote(su)
            sc = kb.v(lambda e: e.tensor_copy(sdv_sb.t[:], S32.t[:]),
                      [su] + sdv_sb.war())
            sdv_sb.wrote(sc); S32.read(sc)
    return cmark, o_stores

def emit_l0dn_tail(st, x_hbm, xout_hbm, W, cmark, o_stores, name="dn"):
    nc, kb, ctx = st.nc, st.kb, st.ctx
    ar = st.arena
    a32 = st.arena32
    onesr, onesr_ld = st.ones_row()
    onesc, onesc_ld = st.ones_col()
    lnab_bf = st.dn_lnab_bf
    wrow = st.dn_wrow
    ar.release(cmark, [])
    mrow = a32.alloc([1, T])

    def rms_row(src_loader, n_tiles, inv_n, eps, store_ops, nm):
        xs = [ar.alloc([128, T]) for _ in range(2)]
        sq = [ar.alloc([128, 512]) for _ in range(2)]
        stat_war = [st.pb[i].begin() for i in range(4)]
        stat_last = [None] * 4
        for i in range(n_tiles):
            j = i % 2
            ld = xs[j].wrote(src_loader(xs[j], i))
            for tq in range(4):
                jj = (i * 4 + tq) % 2
                sqo = kb.act(sq[jj].t[:], xs[j].t[:, tq * 512:(tq + 1) * 512], AF.Square,
                             deps=[ld] + sq[jj].begin())
                sq[jj].wrote(sqo); xs[j].read(sqo)
                pbk = st.pb[tq]
                mmn = pbk.wrote(kb.mm(pbk.t[0:1, :], onesc[:], sq[jj].t[:],
                                      start=(i == 0), stop=(i == n_tiles - 1),
                                      deps=[sqo, onesc_ld] + (stat_war[tq] if i == 0 else [])))
                sq[jj].read(mmn)
                stat_last[tq] = mmn
        mw = mrow.begin()
        evs = []
        for tq in range(4):
            ev = kb.act(mrow.t[0:1, tq * 512:(tq + 1) * 512], st.pb[tq].t[0:1, :],
                        AF.Identity, scale=inv_n, deps=[stat_last[tq]] + mw)
            st.pb[tq].read(ev); mrow.wrote(ev); evs.append(ev)
        r1 = kb.v(lambda e: e.tensor_scalar_add(mrow.t[0:1, :], mrow.t[0:1, :], eps), evs)
        mrow.wrote(r1)
        r2 = kb.act(mrow.t[0:1, :], mrow.t[0:1, :], AF.Sqrt, deps=[r1])
        mrow.wrote(r2)
        r3 = kb.v(lambda e: e.reciprocal(mrow.t[0:1, :], mrow.t[0:1, :]), [r2])
        mrow.wrote(r3)
        rbf = ar.alloc([1, T])
        r4 = kb.v(lambda e: e.tensor_copy(rbf.t[0:1, :], mrow.t[0:1, :]),
                  [r3] + rbf.begin())
        rbf.wrote(r4); mrow.read(r4)
        return rbf, r4, xs, sq

    amark = ar.mark()
    rbf, rop, xs_bufs, sq_a = rms_row(
        lambda buf, i: kb.dma("sync", buf.t[:], W["o_scr"][i * 128:(i + 1) * 128, :],
                              deps=buf.begin() + [o_stores[i * 2 * DN_NCH + k]
                                                  for k in range(2 * DN_NCH)]),
        NCT, 1.0 / H, 1e-6, o_stores, "orms")
    on_t = [ar.alloc([128, T]) for _ in range(NCT)]
    for i in range(NCT):
        ld = xs_bufs[i % 2].wrote(kb.dma("sync", xs_bufs[i % 2].t[:],
                                         W["o_scr"][i * 128:(i + 1) * 128, :],
                                         deps=xs_bufs[i % 2].begin()))
        hw = on_t[i].begin()
        for tq in range(4):
            pbk = st.pb[tq % 2]
            mmb = pbk.wrote(kb.mm(pbk.t[:], wrow["onw"].t[0:1, i * 128:(i + 1) * 128],
                                  rbf.t[0:1, tq * 512:(tq + 1) * 512], start=True, stop=True,
                                  deps=[rop, st.dn_wrow_ld["onw"]] + pbk.begin()))
            rbf.read(mmb)
            tt = kb.v(lambda e, i=i, tq=tq, pbk=pbk, j=i % 2: e.tensor_mul(
                on_t[i].t[:, tq * 512:(tq + 1) * 512],
                xs_bufs[j].t[:, tq * 512:(tq + 1) * 512], pbk.t[:]),
                [ld, mmb] + hw)
            on_t[i].wrote(tt); xs_bufs[i % 2].read(tt); pbk.read(tt)
    wot = [ar.alloc([128, NCT, 128]) for _ in range(2)]
    xrt = [a32.alloc([128, 512]) for _ in range(2)]
    o32 = [a32.alloc([128, 512]) for _ in range(2)]
    x1_stores = {}
    for co in range(NCT):
        j = co % 2
        ldw = wot[j].wrote(kb.dma("gpsimd", wot[j].t[:],
                                  W["woutT"][:, co * 128:(co + 1) * 128].rearrange(
                                      "(a p) n -> p a n", p=128), deps=wot[j].begin()))
        for tq in range(4):
            sl = slice(tq * 512, (tq + 1) * 512)
            ldx = xrt[j].wrote(kb.dma("sync", xrt[j].t[:], x_hbm[co * 128:(co + 1) * 128, sl],
                                      deps=xrt[j].begin()))
            pO = st.pb[2 + tq % 2]
            ow = pO.begin()
            mml = None
            for kt in range(NCT):
                mml = pO.wrote(kb.mm(pO.t[:], wot[j].t[:, kt, :], on_t[kt].t[:, sl],
                                     start=(kt == 0), stop=(kt == NCT - 1),
                                     deps=[ldw] + on_t[kt].rdeps() + (ow if kt == 0 else [])))
                wot[j].read(mml); on_t[kt].read(mml)
            pA = st.pb[4]
            mma = pA.wrote(kb.mm(pA.t[:], onesr[:], lnab_bf.t[0:1, sl], start=True,
                                 stop=True, deps=[st.dn_lnab_op, onesr_ld] + pA.begin()))
            pB = st.pb[5]
            mmbb = pB.wrote(kb.mm(pB.t[:], onesr[:], lnab_bf.t[0:1, T + tq * 512:T + (tq + 1) * 512],
                                  start=True, stop=True,
                                  deps=[st.dn_lnab_op, onesr_ld] + pB.begin()))
            t1 = kb.v(lambda e, j=j, pA=pA: e.tensor_mul(o32[j].t[:], xrt[j].t[:], pA.t[:]),
                      [ldx, mma] + o32[j].begin())
            xrt[j].read(t1); pA.read(t1); o32[j].wrote(t1)
            t2 = kb.v(lambda e, j=j, pB=pB: e.tensor_add(o32[j].t[:], o32[j].t[:], pB.t[:]),
                      [t1, mmbb])
            pB.read(t2); o32[j].wrote(t2)
            t3 = kb.v(lambda e, j=j, pO=pO: e.tensor_add(o32[j].t[:], o32[j].t[:], pO.t[:]),
                      [t2, mml])
            pO.read(t3); o32[j].wrote(t3)
            sd = kb.dma("sync", W["x1_scr"][co * 128:(co + 1) * 128, sl], o32[j].t[:],
                        deps=[t3])
            o32[j].read(sd)
            x1_stores[(co, tq)] = sd
    ar.release(amark, on_t + xs_bufs + wot + sq_a + [rbf])
    bmark = ar.mark()
    rbf2, rop2, xs2, sq_b = rms_row(
        lambda buf, i: kb.dma("gpsimd", buf.t[:], W["x1_scr"][i * 128:(i + 1) * 128, :],
                              deps=buf.begin() + [x1_stores[(i, tq)] for tq in range(4)]),
        NCT, 1.0 / H, 1e-6, None, "mrms")
    h2 = [ar.alloc([128, T]) for _ in range(NCT)]
    for i in range(NCT):
        ld = xs2[i % 2].wrote(kb.dma("gpsimd", xs2[i % 2].t[:],
                                     W["x1_scr"][i * 128:(i + 1) * 128, :],
                                     deps=xs2[i % 2].begin() + [x1_stores[(i, tq)] for tq in range(4)]))
        hw = h2[i].begin()
        for tq in range(4):
            pbk = st.pb[tq % 2]
            mmb = pbk.wrote(kb.mm(pbk.t[:], wrow["mnw"].t[0:1, i * 128:(i + 1) * 128],
                                  rbf2.t[0:1, tq * 512:(tq + 1) * 512], start=True, stop=True,
                                  deps=[rop2, st.dn_wrow_ld["mnw"]] + pbk.begin()))
            rbf2.read(mmb)
            tt = kb.v(lambda e, i=i, tq=tq, pbk=pbk, j=i % 2: e.tensor_mul(
                h2[i].t[:, tq * 512:(tq + 1) * 512],
                xs2[j].t[:, tq * 512:(tq + 1) * 512], pbk.t[:]),
                [ld, mmb] + hw)
            h2[i].wrote(tt); xs2[i % 2].read(tt); pbk.read(tt)
    NGU = 44
    wgu = [ar.alloc([128, NCT, 128]) for _ in range(2)]
    gt = [ar.alloc([128, T]) for _ in range(2)]
    ut = [ar.alloc([128, T]) for _ in range(2)]
    gu_stores = {}
    for ot in range(NGU):
        j = ot % 2
        c0 = ot * 128
        ldg = wgu[0].wrote(kb.dma("gpsimd", wgu[0].t[:],
                                  W["wgT"][:, c0:c0 + 128].rearrange("(a p) n -> p a n", p=128),
                                  deps=wgu[0].begin()))
        ldu = wgu[1].wrote(kb.dma("gpsimd", wgu[1].t[:],
                                  W["wuT"][:, c0:c0 + 128].rearrange("(a p) n -> p a n", p=128),
                                  deps=wgu[1].begin()))
        gw = gt[j].begin(); uw = ut[j].begin()
        for tq in range(4):
            sl = slice(tq * 512, (tq + 1) * 512)
            pbk = st.pb[2 + tq % 2]
            war = pbk.begin()
            mml = None
            for ct in range(NCT):
                mml = pbk.wrote(kb.mm(pbk.t[:], wgu[0].t[:, ct, :], h2[ct].t[:, sl],
                                      start=(ct == 0), stop=(ct == NCT - 1),
                                      deps=[ldg] + h2[ct].rdeps() + (war if ct == 0 else [])))
                h2[ct].read(mml); wgu[0].read(mml)
            ev = kb.act(gt[j].t[:, sl], pbk.t[:], AF.Silu, deps=[mml] + gw)
            pbk.read(ev); gt[j].wrote(ev)
            pbk2 = st.pb[4 + tq % 2]
            war2 = pbk2.begin()
            mml2 = None
            for ct in range(NCT):
                mml2 = pbk2.wrote(kb.mm(pbk2.t[:], wgu[1].t[:, ct, :], h2[ct].t[:, sl],
                                        start=(ct == 0), stop=(ct == NCT - 1),
                                        deps=[ldu] + h2[ct].rdeps() + (war2 if ct == 0 else [])))
                h2[ct].read(mml2); wgu[1].read(mml2)
            ev2 = kb.v(lambda e, j=j, sl=sl, pbk2=pbk2: e.tensor_mul(
                ut[j].t[:, sl], gt[j].t[:, sl], pbk2.t[:]), [mml2, ev] + uw)
            pbk2.read(ev2); ut[j].wrote(ev2); gt[j].read(ev2)
        sd = kb.dma("sync", W["gu_scr"][c0:c0 + 128, :], ut[j].t[:], deps=ut[j].rdeps())
        ut[j].read(sd)
        gu_stores[ot] = sd
    ar.release(bmark, h2 + gt + ut + xs2 + [rbf2] + wgu + sq_b)
    guch = ar.alloc([128, NGU, 512])
    wdt = [ar.alloc([128, NGU, 128]) for _ in range(2)]
    xt2 = [a32.alloc([128, 512]) for _ in range(2)]
    for tq in range(4):
        sl = slice(tq * 512, (tq + 1) * 512)
        ldgu = guch.wrote(kb.dma("sync", guch.t[:],
                                 W["gu_scr"][:, sl].rearrange("(a p) n -> p a n", p=128),
                                 deps=guch.begin() + list(gu_stores.values())))
        for co in range(NCT):
            j = co % 2
            ldw = wdt[j].wrote(kb.dma("gpsimd", wdt[j].t[:],
                                      W["wdT"][:, co * 128:(co + 1) * 128].rearrange(
                                          "(a p) n -> p a n", p=128), deps=wdt[j].begin()))
            ldx1 = xrt[j].wrote(kb.dma("sync", xrt[j].t[:],
                                       W["x1_scr"][co * 128:(co + 1) * 128, sl],
                                       deps=xrt[j].begin() + [x1_stores[(co, tq)]]))
            ldx0 = xt2[j].wrote(kb.dma("sync", xt2[j].t[:], x_hbm[co * 128:(co + 1) * 128, sl],
                                       deps=xt2[j].begin()))
            pO = st.pb[2 + co % 2]
            ow = pO.begin()
            mml = None
            for kt in range(NGU):
                mml = pO.wrote(kb.mm(pO.t[:], wdt[j].t[:, kt, :], guch.t[:, kt, :],
                                     start=(kt == 0), stop=(kt == NGU - 1),
                                     deps=[ldw, ldgu] + (ow if kt == 0 else [])))
                wdt[j].read(mml); guch.read(mml)
            t1 = kb.v(lambda e, j=j, pO=pO: e.tensor_add(o32[j].t[:], xrt[j].t[:], pO.t[:]),
                      [mml, ldx1] + o32[j].begin())
            pO.read(t1); xrt[j].read(t1); o32[j].wrote(t1)
            t2 = kb.v(lambda e, j=j: e.tensor_add(o32[j].t[:], o32[j].t[:], xt2[j].t[:]),
                      [t1, ldx0])
            xt2[j].read(t2); o32[j].wrote(t2)
            sd = kb.dma("sync", xout_hbm[co * 128:(co + 1) * 128, sl], o32[j].t[:],
                        deps=[t2])
            o32[j].read(sd)

def stage_l0dn(inp):
    c = np.ascontiguousarray
    w = {}
    w["wqT"] = c(inp["L0_attn_a_Wq"].T)
    w["wkT"] = c(inp["L0_attn_a_Wk"].T)
    w["wvT"] = c(inp["L0_attn_a_Wv"].T)
    w["conv_q"] = c(inp["L0_attn_a_conv_q"])
    w["conv_k"] = c(inp["L0_attn_a_conv_k"])
    w["conv_v"] = c(inp["L0_attn_a_conv_v"])
    w["wbT"] = c(inp["L0_attn_a_Wbeta"].T)
    w["anw"] = c(inp["L0_attn_a_attn_norm_w"][None, :])
    w["onw"] = c(inp["L0_attn_a_out_norm_w"][None, :])
    w["mnw"] = c(inp["L0_attn_a_mlp_norm_w"][None, :])
    w["woutT"] = c(inp["L0_attn_a_Wout"].T)
    w["wgT"] = c(inp["L0_attn_a_Wgate"].T)
    w["wuT"] = c(inp["L0_attn_a_Wup"].T)
    w["wdT"] = c(inp["L0_attn_a_Wdown"].T)
    return w

def build_l0dn_program(trunc=None):
    nc = bass.Bass("TRN2", target_bir_lowering=False, debug=False)
    xin = nc.dram_tensor("xin", [H, T], F32, kind="ExternalInput").ap()
    W = {}
    for nm, shp in (("wqT", [H, H]), ("wkT", [H, H]), ("wvT", [H, H]),
                    ("conv_q", [H, 4]), ("conv_k", [H, 4]), ("conv_v", [H, 4]),
                    ("wbT", [H, 32]), ("anw", [1, H]), ("onw", [1, H]),
                    ("mnw", [1, H]), ("woutT", [H, H]), ("wgT", [H, 5632]),
                    ("wuT", [H, 5632]), ("wdT", [5632, H])):
        W[nm] = nc.dram_tensor(nm, shp, F32, kind="ExternalInput").ap()
    for nm in ("q_scr", "k_scr", "v_scr", "o_scr"):
        W[nm] = nc.dram_tensor("dn_" + nm, [H, T], BF16, kind="Internal").ap()
    W["x1_scr"] = nc.dram_tensor("dn_x1scr", [H, T], F32, kind="Internal").ap()
    W["gu_scr"] = nc.dram_tensor("dn_guscr", [5632, T], BF16, kind="Internal").ap()
    xout = nc.dram_tensor("xout", [H, T], F32, kind="ExternalOutput").ap()
    ctx = ExitStack()
    with ctx:
        kb = KB(nc)
        st = St(nc, kb, ctx, arena_words=55296, arena32_words=12288)
        cmark, o_stores = emit_l0dn(st, xin, xout, W)
        emit_l0dn_tail(st, xin, xout, W, cmark, o_stores)
        build_l0dn_program.marks = dict(getattr(kb, "marks", {}))
        build_l0dn_program.nops = len(kb.ops)
        if trunc is not None:
            kb.ops = kb.ops[:trunc]
        with nc.Block() as block:
            kb.emit(block, ctx)
    return nc

def kernel_multilaunch(**inputs):
    c = np.ascontiguousarray
    x = np.asarray(inputs["L0_attn_a_hidden_states"])
    B = x.shape[0]
    w_dn = stage_l0dn(inputs)
    w_ffn = {"wgt": c(np.asarray(inputs["L0_ffn_W_gate"]).T),
             "wqkvt": c(np.asarray(inputs["L0_ffn_W_qkv"]).T)}
    w_mha = {"wqkvt": c(np.asarray(inputs["L0_attn_b_W_qkv"]).T),
             "bqkv": c(np.asarray(inputs["L0_attn_b_b_qkv"]).reshape(-1, 1)),
             "woutt": c(np.asarray(inputs["L0_attn_b_W_out"]).T),
             "bout": c(np.asarray(inputs["L0_attn_b_b_out"]).reshape(-1, 1))}
    w_ssd = stage_l1ssd(inputs)
    w_moe = stage_l1ffn(inputs)
    w_gla = {"wq": c(np.asarray(inputs["L1_attn_b_Wq"])),
             "wk": c(np.asarray(inputs["L1_attn_b_Wk"])),
             "wv": c(np.asarray(inputs["L1_attn_b_Wv"])),
             "wo": c(np.asarray(inputs["L1_attn_b_Wo"])),
             "wg": c(np.asarray(inputs["L1_attn_b_Wg"])),
             "bg": c(np.asarray(inputs["L1_attn_b_bg"]).reshape(-1, 1)),
             "bo": c(np.asarray(inputs["L1_attn_b_bo"]).reshape(-1, 1))}
    p_dn = build_l0dn_program()
    p_ffn = build_l0ffn_program()
    p_mha = build_mha_program()
    p_ssd = build_l1ssd_program()
    p_moe = build_l1ffn_program()
    p_gla = build_l1gla_program()
    seq = [(p_dn, w_dn), (p_ffn, w_ffn), (p_mha, w_mha), (p_ffn, w_ffn),
           (p_ssd, w_ssd), (p_moe, w_moe), (p_gla, w_gla), (p_moe, w_moe)]
    xt = [c(x[b].T) for b in range(B)]
    for nc_prog, wmap in seq:
        in_maps = [dict(wmap, xin=xt[b]) for b in range(B)]
        res = run_bass_kernel_spmd(nc_prog, in_maps, core_ids=list(range(B)))
        xt = [np.asarray(res.results[b]["xout"]) for b in range(B)]
    return np.stack([xt[b].T for b in range(B)], 0).astype(np.float32)

def build_fused_program():
    nc = bass.Bass("TRN2", target_bir_lowering=False, debug=False)
    def dr(nm, shp, dt=F32, kind="ExternalInput"):
        return nc.dram_tensor(nm, shp, dt, kind=kind).ap()
    xin = dr("xin", [H, T])
    xout = dr("xout", [H, T], F32, "ExternalOutput")
    s1 = dr("xs1", [H, T], F32, "Internal")
    s2 = dr("xs2", [H, T], F32, "Internal")
    Wdn = {}
    for nm, shp in (("wqT", [H, H]), ("wkT", [H, H]), ("wvT", [H, H]),
                    ("conv_q", [H, 4]), ("conv_k", [H, 4]), ("conv_v", [H, 4]),
                    ("wbT", [H, 32]), ("anw", [1, H]), ("onw", [1, H]),
                    ("mnw", [1, H]), ("woutT", [H, H]), ("wgT", [H, 5632]),
                    ("wuT", [H, 5632]), ("wdT", [5632, H])):
        Wdn[nm] = dr("dn_" + nm, shp)
    for nm in ("q_scr", "k_scr", "v_scr", "o_scr"):
        Wdn[nm] = dr("dnx_" + nm, [H, T], BF16, "Internal")
    Wdn["x1_scr"] = dr("dnx_x1scr", [H, T], F32, "Internal")
    Wdn["gu_scr"] = dr("dnx_guscr", [5632, T], BF16, "Internal")
    ffn_wgt = dr("ffn_wgt", [H, 16])
    ffn_wqkvt = dr("ffn_wqkvt", [H, 3 * H])
    mha_wq = dr("mha_wqkvt", [H, 3 * H])
    mha_bq = dr("mha_bqkv", [3 * H, 1])
    mha_wo = dr("mha_woutt", [H, H])
    mha_bo = dr("mha_bout", [H, 1])
    mha_of = dr("mhax_of", [H, T], BF16, "Internal")
    Wssd = {}
    Wssd["win"] = dr("ssd_win", [H, 8512])
    Wssd["convwT"] = dr("ssd_convwT", [4352, 4])
    Wssd["convb"] = dr("ssd_convb", [4352, 1])
    Wssd["dtb"] = dr("ssd_dtb", [64, 1])
    Wssd["alog"] = dr("ssd_alog", [64, 1])
    Wssd["dful"] = dr("ssd_dful", [128, 32])
    Wssd["nsc"] = dr("ssd_nsc", [128, 32])
    Wssd["wout"] = dr("ssd_wout", [4096, H])
    for nm in ("x_scr", "z_scr", "y_scr", "yg_scr"):
        Wssd[nm] = dr("ssdx_" + nm, [4096, T], BF16, "Internal")
    Wmoe = {}
    for nm, shp in (("gwpT", [H, 64]), ("muw", [8, H, 8]), ("gwT", [8, H, NI]),
                    ("uwT", [8, H, NI]), ("dwT", [8, NI, H]), ("qaguT", [8, H, 128]),
                    ("qadT", [8, NI, 64]), ("qbg", [8, 64, NI]), ("qbu", [8, 64, NI]),
                    ("qbd", [8, 64, H]), ("gsc", [8, NI]), ("usc", [8, NI]),
                    ("dsc", [8, H]), ("shgT", [H, ISH]), ("shuT", [H, ISH]),
                    ("shdT", [ISH, H])):
        Wmoe[nm] = dr("moe_" + nm, shp)
    Wmoe["gu_scr"] = dr("moex_gu_scr", [ISH + 8 * NI, T], BF16, "Internal")
    Wmoe["softT_scr"] = dr("moex_softT_scr", [8, 8 * T], BF16, "Internal")
    Wmoe["qbd_scr"] = dr("moex_qbd_scr", [8, 64, H], BF16, "Internal")
    Wgla = {}
    for nm in ("wq", "wk", "wv", "wo"):
        Wgla[nm] = dr("gla_" + nm, [H, H])
    Wgla["wg"] = dr("gla_wg", [H, 16])
    Wgla["bg"] = dr("gla_bg", [16, 1])
    Wgla["bo"] = dr("gla_bo", [H, 1])
    Wgla["oscr"] = dr("glax_oscr", [H, T], BF16, "Internal")
    gctx = ExitStack()
    with gctx:
        kb = KB(nc)
        st = St(nc, kb, gctx)
        st.gctx = gctx

        def phase(i, aw, a32w):
            st.phase = i
            sctx = ExitStack()
            st.ctx = sctx
            st.arena = Arena(sctx, nc, f"arena_p{i}", aw) if aw else None
            st.arena32 = Arena(sctx, nc, f"arena32_p{i}", a32w, F32) if a32w else None
            return sctx

        sc = phase(0, 55296, 12288)
        cmark, o_st = emit_l0dn(st, xin, s1, Wdn, name="p0dn")
        emit_l0dn_tail(st, xin, s1, Wdn, cmark, o_st, name="p0dn")
        sc.close(); kb.fence()
        sc = phase(1, 0, 0)
        emit_l0ffn(st, s1, s2, ffn_wgt, ffn_wqkvt, name="p1mf")
        sc.close(); kb.fence()
        sc = phase(2, 0, 0)
        emit_mha(st, s2, s1, mha_wq, mha_bq, mha_wo, mha_bo, mha_of, name="p2mha")
        sc.close(); kb.fence()
        sc = phase(3, 0, 0)
        emit_l0ffn(st, s1, s2, ffn_wgt, ffn_wqkvt, name="p3mf")
        sc.close(); kb.fence()
        sc = phase(4, 53248, 15872)
        emit_l1ssd(st, s2, s1, Wssd, name="p4ssd")
        sc.close(); kb.fence()
        sc = phase(5, 85504, 6272)
        emit_l1ffn(st, s1, s2, Wmoe, name="p5moe")
        sc.close(); kb.fence()
        sc = phase(6, 46080, 8960)
        emit_l1gla(st, s2, s1, Wgla, name="p6gla")
        sc.close(); kb.fence()
        sc = phase(7, 85504, 6272)
        emit_l1ffn(st, s1, xout, Wmoe, name="p7moe")
        sc.close()
        st.ctx = gctx
        build_fused_program.nops = len(kb.ops)
        with nc.Block() as block:
            kb.emit(block, gctx)
    return nc

def stage_all(inputs):
    c = np.ascontiguousarray
    w = {}
    for k, v in stage_l0dn(inputs).items():
        w["dn_" + k] = v
    w["ffn_wgt"] = c(np.asarray(inputs["L0_ffn_W_gate"]).T)
    w["ffn_wqkvt"] = c(np.asarray(inputs["L0_ffn_W_qkv"]).T)
    w["mha_wqkvt"] = c(np.asarray(inputs["L0_attn_b_W_qkv"]).T)
    w["mha_bqkv"] = c(np.asarray(inputs["L0_attn_b_b_qkv"]).reshape(-1, 1))
    w["mha_woutt"] = c(np.asarray(inputs["L0_attn_b_W_out"]).T)
    w["mha_bout"] = c(np.asarray(inputs["L0_attn_b_b_out"]).reshape(-1, 1))
    for k, v in stage_l1ssd(inputs).items():
        w["ssd_" + k] = v
    for k, v in stage_l1ffn(inputs).items():
        w["moe_" + k] = v
    w["gla_wq"] = c(np.asarray(inputs["L1_attn_b_Wq"]))
    w["gla_wk"] = c(np.asarray(inputs["L1_attn_b_Wk"]))
    w["gla_wv"] = c(np.asarray(inputs["L1_attn_b_Wv"]))
    w["gla_wo"] = c(np.asarray(inputs["L1_attn_b_Wo"]))
    w["gla_wg"] = c(np.asarray(inputs["L1_attn_b_Wg"]))
    w["gla_bg"] = c(np.asarray(inputs["L1_attn_b_bg"]).reshape(-1, 1))
    w["gla_bo"] = c(np.asarray(inputs["L1_attn_b_bo"]).reshape(-1, 1))
    return w

def kernel(**inputs):
    c = np.ascontiguousarray
    x = np.asarray(inputs["L0_attn_a_hidden_states"])
    B = x.shape[0]
    w = stage_all(inputs)
    nc = build_fused_program()
    in_maps = [dict(w, xin=c(x[b].T)) for b in range(B)]
    res = run_bass_kernel_spmd(nc, in_maps, core_ids=list(range(B)))
    return np.stack([np.asarray(res.results[b]["xout"]).T for b in range(B)], 0).astype(np.float32)
